# version 06
# speedup vs baseline: 1.2354x; 1.0129x over previous
import numpy as np
import concourse.bass as bass
import concourse.mybir as mybir
from concourse.bass_utils import run_bass_kernel_spmd

F32 = mybir.dt.float32
BF16 = mybir.dt.bfloat16
AF = mybir.ActivationFunctionType
ALU = mybir.AluOpType
AX = mybir.AxisListType.X

B, T, D = 32, 2048, 1024
H, HD = 16, 64
E, I_FF = 8, 2816
CLEN = 16
NCORES = 8
BL = B // NCORES
PL = T // NCORES
NTOK = BL * T
KC = D // 128
IC = I_FF // 128
ROPE_BASE = 10000.0
SG = 1024
NSG = NTOK // SG

STAGE_N = 4
DBG = False


class _Dummy:
    def __getitem__(self, k):
        return self

    def __getattr__(self, name):
        return lambda *a, **kw: self

    def __call__(self, *a, **kw):
        return self


DUMMY = _Dummy()


class Builder:
    def __init__(self, real, cps=None):
        self.real = real
        self.cnt = {}
        self.cp = {} if cps is None else cps
        self.sem = {}
        self.nc = None
        self.ctx = {}

    def mark(self, name, sem):
        val = self.cnt.get(sem, 0)
        if self.real:
            assert self.cp.get(name) == (sem, val), \
                f"checkpoint drift {name}: pass1={self.cp.get(name)} pass2={(sem, val)}"
        else:
            assert name not in self.cp, f"dup checkpoint {name}"
            self.cp[name] = (sem, val)

    def bump(self, sem, n):
        self.cnt[sem] = self.cnt.get(sem, 0) + n

    def t(self, name):
        return self.ctx[name] if self.real else DUMMY


SELF_SEM = {"vector": "V", "scalar": "S", "gpsimd": "G"}


class Eng:
    def __init__(self, b, e, name=None):
        self.b = b
        self.e = e
        self.name = name

    def wait(self, cp_name):
        b = self.b
        if b.real:
            sem, val = b.cp[cp_name]
            if val > 0:
                self.e.wait_ge(b.sem[sem], val)
        return self

    def op(self, name, *a, inc=None, **kw):
        b = self.b
        if inc is None:
            inc = []
        elif isinstance(inc, tuple):
            inc = [inc]
        own = SELF_SEM.get(self.name)
        auto = own is not None and name != "dma_start"
        ups = {}
        if auto:
            ups[own] = 1
        for s, n in inc:
            if auto and s == own:
                continue
            ups[s] = n
        if b.real:
            inst = getattr(self.e, name)(*a, **kw)
            if own is not None:
                prev = b.cnt.get(own, 0)
                if prev > 0:
                    inst._wait_ge(b.sem[own], prev)
            for s, n in ups.items():
                inst.then_inc(b.sem[s], n)
        for s, n in ups.items():
            b.bump(s, n)

    def dma(self, out, in_, inc, **kw):
        self.op("dma_start", out=out, in_=in_, inc=inc, **kw)

    def pid256(self):
        if not self.b.real:
            return 0
        if not hasattr(self, "_pid256"):
            self._pid256 = self.e.snap(self.e.partition_id() * 256, min_val=0,
                                       max_val=(NCORES - 1) * 256,
                                       guaranteed_mod_val=256)
        return self._pid256

    def mm(self, out, lhsT, rhs, start=True, stop=True, inc=None, **kw):
        self.op("matmul", out, lhsT, rhs, start=start, stop=stop, inc=inc, **kw)


def host_constants():
    import ml_dtypes
    c = {}
    c["ident_bf"] = np.eye(128).astype(ml_dtypes.bfloat16)
    c["ident_f32"] = np.eye(128, dtype=np.float32)
    c["ones_row_bf"] = np.ones((1, 128), dtype=ml_dtypes.bfloat16)
    kk, qq = np.meshgrid(np.arange(128), np.arange(128), indexing="ij")
    c["causal_keep_bf"] = (kk <= qq).astype(ml_dtypes.bfloat16)
    theta = 1.0 / ROPE_BASE ** (np.arange(0, HD, 2, dtype=np.float64) / HD)
    blocks = np.zeros((KC, 128, 128), dtype=np.float64)
    for ch in range(KC):
        for hh in range(2):
            h = 2 * ch + hh
            base = hh * 64
            cos, sin = np.cos(h * theta), np.sin(h * theta)
            for i in range(HD // 2):
                a2, b2 = base + 2 * i, base + 2 * i + 1
                blocks[ch, a2, a2] = cos[i]
                blocks[ch, b2, a2] = -sin[i]
                blocks[ch, a2, b2] = sin[i]
                blocks[ch, b2, b2] = cos[i]
    c["rope_blocks"] = blocks.reshape(KC * 128, 128).astype(np.float32)
    tp, t2 = np.meshgrid(np.arange(128), np.arange(128), indexing="ij")
    c["um4_bf"] = (((tp // 32) == (t2 // 32)) & ((tp % 32) <= (t2 % 32))).astype(ml_dtypes.bfloat16)
    return c


SEMS = ["T", "V", "S", "G", "DQ", "DS0", "DS1", "DG", "DG0", "DG1", "DW0", "DW1", "DX0", "DX1", "C", "DV"]


def build_ctx(b):
    nc = b.nc
    ctx = b.ctx

    def din(name, shape, dtype=F32):
        ctx[name] = nc.dram_tensor(name, list(shape), dtype, kind="ExternalInput").ap()

    din("x", (NTOK, D))
    din("W_attn", (D, 3 * D)); din("b_attn", (1, 3 * D))
    din("W_proj", (D, D)); din("b_proj", (1, D))
    din("q_c", (CLEN, D)); din("beta", (1, H))
    din("Wq", (D, D)); din("bq", (1, D)); din("Wk", (D, D)); din("bk", (1, D))
    din("Wv", (D, D)); din("bv", (1, D)); din("Wo", (D, D)); din("bo", (1, D))
    din("router_w", (E, D))
    din("gate_w", (E * I_FF, D)); din("up_w", (E * I_FF, D)); din("down_w", (E * D, I_FF))
    din("ident_bf", (128, 128), BF16); din("ident_f32", (128, 128), F32)
    din("ones_row_bf", (1, 128), BF16)
    din("causal_keep_bf", (128, 128), BF16)
    din("rope_blocks", (KC * 128, 128), F32)
    din("um4_bf", (128, 128), BF16)

    ctx["out"] = nc.dram_tensor("out", [B * PL, D], F32, kind="ExternalOutput").ap()

    def dint(name, shape, dtype):
        ctx[name] = nc.dram_tensor(name, list(shape), dtype, kind="Internal").ap()

    if DBG:
        for nm, shp, dt in [("dbg_xln", (128, 8192), BF16), ("dbg_small", (128, 512), F32),
                            ("dbg_wtet", (1, 8192), BF16), ("dbg_ys", (128, 8192), F32)]:
            ctx[nm] = nc.dram_tensor(nm, list(shp), dt, kind="ExternalOutput").ap()
        ctx["dbg_at"] = nc.dram_tensor("dbg_at", [128, 2048], F32, kind="ExternalOutput").ap()
        ctx["dbg_stat"] = nc.dram_tensor("dbg_stat", [128, 16], F32, kind="ExternalOutput").ap()
        ctx["dbg_axs"] = nc.dram_tensor("dbg_axs", [128, 2048], BF16, kind="ExternalOutput").ap()
        ctx["dbg_x1"] = nc.dram_tensor("dbg_x1", [128, 1024], F32, kind="ExternalOutput").ap()
        ctx["dbg_axln"] = nc.dram_tensor("dbg_axln", [256, 2048], BF16, kind="ExternalOutput").ap()
        ctx["dbg_aq"] = nc.dram_tensor("dbg_aq", [128, 12480], BF16, kind="ExternalOutput").ap()
        ctx["dbg_ay"] = nc.dram_tensor("dbg_ay", [128, 16384], BF16, kind="ExternalOutput").ap()
        ctx["dbg_aw"] = nc.dram_tensor("dbg_aw", [128, 24576], BF16, kind="ExternalOutput").ap()
        ctx["dbg_bay"] = nc.dram_tensor("dbg_bay", [128, 16384], BF16, kind="ExternalOutput").ap()
        ctx["dbg_bapt"] = nc.dram_tensor("dbg_bapt", [128, 4096], BF16, kind="ExternalOutput").ap()
        ctx["dbg_bostg"] = nc.dram_tensor("dbg_bostg", [128, 2048], BF16, kind="ExternalOutput").ap()
        ctx["dbg_bgstg"] = nc.dram_tensor("dbg_bgstg", [128, 2048], BF16, kind="ExternalOutput").ap()
        ctx["dbg_bft"] = nc.dram_tensor("dbg_bft", [128, 512], F32, kind="ExternalOutput").ap()
        ctx["dbg_bsm"] = nc.dram_tensor("dbg_bsm", [128, 512], F32, kind="ExternalOutput").ap()
        ctx["dbg_bsmb"] = nc.dram_tensor("dbg_bsmb", [128, 512], BF16, kind="ExternalOutput").ap()
        ctx["dbg_bx"] = nc.dram_tensor("dbg_bx", [128, 2048], F32, kind="ExternalOutput").ap()
        ctx["dbg_wkt"] = nc.dram_tensor("dbg_wkt", [D, D], BF16, kind="ExternalOutput").ap()
    dint("x1_spill", (NTOK, D), F32)
    dint("x2_spill", (NTOK, D), F32)
    for gq in range(4):
        dint(f"x2_ag{gq}", (NCORES * 2048, D), F32)
    dint("x3_local", (NTOK, D), F32)
    dint("x2_local", (NTOK, D), F32)
    dint("xln_spill", (KC * 128, T), BF16)
    dint("wbf_stage", (D, 3 * D), BF16)
    dint("wproj_bf", (D, D), BF16)
    dint("gate_t", (E * IC * 128, D), BF16)
    dint("up_t", (E * IC * 128, D), BF16)
    dint("down_t", (E * IC * 128, D), BF16)
    dint("gate_bf", (E * I_FF, D), BF16)
    dint("up_bf", (E * I_FF, D), BF16)
    dint("down_bf", (E * D, I_FF), BF16)
    dint("wq_t", (D, D), BF16)
    dint("wk_t", (D, D), BF16)
    dint("wv_t", (D, D), BF16)
    dint("wo_t", (D, D), BF16)

    def asb(name, shape, dtype):
        ctx[name] = nc.alloc_sbuf_tensor(name, list(shape), dtype).ap()

    asb("AW", (128, 24576), BF16)
    asb("AY", (128, 16384), BF16)
    asb("AQ", (128, 12480), BF16)
    asb("AXS", (128, 2048), BF16)
    asb("AF1", (128, 8192), F32)
    asb("AT", (128, 2048), F32)
    asb("APT", (128, 4096), BF16)
    asb("CONST", (128, 16), F32)
    asb("IDB", (128, 128), BF16)
    asb("IDF", (128, 128), F32)
    asb("ONESR", (1, 128), BF16)
    asb("CAUSAL", (128, 128), BF16)
    asb("UM4", (128, 128), BF16)
    asb("BIASA", (128, 24), F32)
    asb("BIASP", (128, 8), F32)
    asb("BIASB", (128, 32), F32)
    asb("QSCALE", (128, 8), F32)
    asb("QCB", (128, 512), BF16)
    asb("ROUTER", (128, 64), BF16)
    asb("BR", (1, 3072), F32)
    asb("ROWB", (1, 2432), BF16)
    asb("ROWB2", (1, 1024), BF16)
    asb("STAT", (128, 16), F32)
    asb("FT", (128, 512), F32)
    asb("SMALL", (128, 512), F32)
    asb("SMALLB", (128, 512), BF16)
    asb("WSTG", (128, 2048), BF16)
    asb("GSTG", (128, 2048), BF16)
    asb("OSTG", (128, 2048), BF16)

    ctx["PS"] = nc.alloc_psum_tensor("PSA", [128, 4096], F32).ap()
    ctx["PSB"] = ctx["PS"].bitcast(BF16)

    for s in SEMS:
        b.sem[s] = nc.semaphore(s).__enter__()


def psf(b, bank, off=0, w=512):
    return b.t("PS")[:, bank * 512 + off: bank * 512 + off + w]


def psbf(b, bank, off=0, w=1024):
    return b.t("PSB")[:, bank * 1024 + off: bank * 1024 + off + w]


def aptf(b):
    return b.t("APT").bitcast(F32)


def build_program(b):
    if b.real:
        b.nc = bass.Bass("TRN2", target_bir_lowering=False, dynamic_dma_scratch_size=16384)
        build_ctx(b)
        CP = b.ctx["CONST"]
        for col, v in CONST_VALS:
            b.nc.const_aps.register(F32, v, CP[:, col:col + 1])
        block = b.nc.Block()
        block.__enter__()

        @block.sync
        def _(e):
            emit_all(b, "sync", Eng(b, e, "sync"))

        @block.gpsimd
        def _(e):
            emit_all(b, "gpsimd", Eng(b, e, "gpsimd"))

        @block.vector
        def _(e):
            emit_all(b, "vector", Eng(b, e, "vector"))

        @block.scalar
        def _(e):
            emit_all(b, "scalar", Eng(b, e, "scalar"))

        @block.tensor
        def _(e):
            emit_all(b, "tensor", Eng(b, e, "tensor"))

        block.__exit__(None, None, None)
        return b.nc
    for eng in ["sync", "gpsimd", "vector", "scalar", "tensor"]:
        emit_all(b, eng, Eng(b, None, eng))
    return None


CONST_VALS = [(0, 1e-5), (1, -0.5), (2, 1.0 / D), (3, 0.125), (4, -30000.0), (5, 1.0),
              (6, 0.5), (7, -0.125)]


def emit_all(b, eng, E_):
    phase_consts(b, eng, E_)
    phase_wfold(b, eng, E_)
    for s in range(BL):
        attn_a_seq(b, eng, E_, s)
    phase_weights(b, eng, E_)
    if STAGE_N >= 2:
        phase_ffn(b, eng, E_, app=1)
    if STAGE_N >= 3:
        phase_reshard(b, eng, E_)
        phase_attn_b(b, eng, E_)
    if STAGE_N >= 4:
        phase_ffn(b, eng, E_, app=2)
    stage_tap(b, eng, E_)


def phase_consts(b, eng, E_):
    if eng == "vector":
        for col, v in CONST_VALS:
            E_.op("memset", b.t("CONST")[:, col:col + 1], v)
        E_.op("memset", b.t("QCB")[:, :], 0.0)
        for ti in range(16):
            base = 8192 + ti * 264
            for hcv in range(4):
                E_.op("memset", b.t("AQ")[:, base + hcv * 66 + 64: base + hcv * 66 + 66], 1.0)
        E_.op("memset", b.t("SMALLB")[:, 496:498], 1.0)
        E_.op("memset", b.t("SMALL")[:, 64:96], 0.0, inc=("V", 1))
        b.mark("pc.v", "V")
    if eng == "sync":
        for name, src in [("IDB", "ident_bf"), ("IDF", "ident_f32"),
                          ("CAUSAL", "causal_keep_bf"), ("UM4", "um4_bf")]:
            E_.dma(b.t(name)[:, :], b.t(src), inc=("DQ", 16))
        E_.dma(b.t("ONESR")[0:1, :], b.t("ones_row_bf"), inc=("DQ", 16))
        b.mark("pc.dma", "DQ")


def phase_wfold(b, eng, E_):
    NFC = 16

    if eng == "gpsimd":
        for kc in range(KC):
            E_.dma(b.t("wbf_stage")[kc * 128:(kc + 1) * 128, :],
                   b.t("W_attn")[kc * 128:(kc + 1) * 128, :], inc=("DG", 16))
            b.mark(f"wf.cw{kc}", "DG")
        for blk in range(KC):
            E_.dma(b.t("AY")[:, blk * 128:(blk + 1) * 128],
                   b.t("rope_blocks")[blk * 128:(blk + 1) * 128, :], inc=("DG", 16))
        b.mark("wf.casts", "DG")
        E_.wait("wf.casts")
        for kc in range(KC):
            E_.dma(b.t("wproj_bf")[kc * 128:(kc + 1) * 128, :],
                   b.t("W_proj")[kc * 128:(kc + 1) * 128, :], inc=("DG", 16))
        b.mark("wf.castp", "DG")
        for qr in range(4):
            E_.dma(b.t("ROWB2")[0:1, qr * 256:(qr + 1) * 256],
                   b.t("b_attn")[0:1, 2048 + qr * 256: 2048 + (qr + 1) * 256], inc=("DG", 16))
        b.mark("wf.bv", "DG")
        if STAGE_N >= 2:
            for e in range(E):
                E_.dma(b.t("gate_bf")[e * I_FF:(e + 1) * I_FF, :],
                       b.t("gate_w")[e * I_FF:(e + 1) * I_FF, :], inc=("DG", 16))
                b.mark(f"pc2.g{e}", "DG")
                E_.dma(b.t("up_bf")[e * I_FF:(e + 1) * I_FF, :],
                       b.t("up_w")[e * I_FF:(e + 1) * I_FF, :], inc=("DG", 16))
                b.mark(f"pc2.u{e}", "DG")
                E_.dma(b.t("down_bf")[e * D:(e + 1) * D, :],
                       b.t("down_w")[e * D:(e + 1) * D, :], inc=("DG", 16))
                b.mark(f"pc2.d{e}", "DG")

    if eng == "sync":
        E_.wait("pc.dma")
        for kc in range(KC):
            E_.wait(f"wf.cw{kc}")
            E_.dma(b.t("AW")[:, kc * 3072:(kc + 1) * 3072],
                   b.t("wbf_stage")[kc * 128:(kc + 1) * 128, :], inc=("DQ", 16))
        b.mark("wf.vcols", "DQ")
        E_.wait("wf.vcols")
        E_.dma(b.t("BR")[0:1, :], b.t("b_attn"), inc=("DQ", 16))
        b.mark("wf.battn_row", "DQ")

    if eng == "tensor":
        E_.wait("wf.casts")
        E_.wait("wf.vcols")
        for fc in range(NFC):
            par = fc % 2
            if fc >= 2:
                E_.wait(f"wf.tev.{fc-2}")
            for ic in range(KC):
                E_.op("transpose", psbf(b, 4 + 2 * par, ic * 256, 128),
                      b.t("AW")[:, ic * 3072 + fc * 128: ic * 3072 + (fc + 1) * 128],
                      b.t("IDB")[:, :], inc=("T", 1) if ic == KC - 1 else None)
            b.mark(f"wf.tr.{fc}", "T")
            E_.wait(f"wf.tev.{fc}")
            if fc >= 2:
                E_.wait(f"wf.ev.{fc-2}")
            for ic in range(KC):
                E_.mm(psf(b, par * 2 + ic // 4, (ic % 4) * 128, 128),
                      b.t("WSTG")[:, par * 1024 + ic * 128: par * 1024 + (ic + 1) * 128],
                      b.t("AY")[:, (fc % 8) * 128:((fc % 8) + 1) * 128],
                      inc=("T", 1) if ic == KC - 1 else None)
            b.mark(f"wf.mm.{fc}", "T")

    if eng == "vector":
        for fc in range(NFC):
            E_.wait(f"wf.tr.{fc}")
            if fc >= 2:
                E_.wait(f"wf.mm.{fc-2}")
            par = fc % 2
            for ic in range(KC):
                E_.op("tensor_copy",
                      b.t("WSTG")[:, par * 1024 + ic * 128: par * 1024 + (ic + 1) * 128],
                      psbf(b, 4 + 2 * par, ic * 256, 128),
                      inc=("V", 1) if ic == KC - 1 else None)
            b.mark(f"wf.tev.{fc}", "V")

    if eng == "scalar":
        for fc in range(NFC):
            E_.wait(f"wf.mm.{fc}")
            par = fc % 2
            for ic in range(KC):
                E_.op("copy",
                      b.t("AW")[:, ic * 3072 + fc * 128: ic * 3072 + fc * 128 + 128],
                      psf(b, par * 2 + ic // 4, (ic % 4) * 128, 128),
                      inc=("S", 1) if ic == KC - 1 else None)
            b.mark(f"wf.ev.{fc}", "S")

    for oc in range(24):
        tg = f"wf.ba.{oc}"
        if oc < 16:
            if eng == "tensor":
                E_.wait("wf.battn_row")
                E_.wait("pc.dma")
                if oc > 0:
                    E_.wait(f"wf.ba.{oc-1}.e3")
                E_.op("transpose", psf(b, 7, 0, 1), b.t("BR")[0:1, oc * 128:(oc + 1) * 128],
                      b.t("IDF")[0:1, 0:1], inc=("T", 1))
                b.mark(f"{tg}.t1", "T")
                E_.wait(f"{tg}.e1")
                E_.mm(psf(b, 7, 128, 128)[0:1, :], b.t("SMALLB")[:, 497:498],
                      b.t("AY")[:, (oc % 8) * 128:((oc % 8) + 1) * 128], inc=("T", 1))
                b.mark(f"{tg}.mmf", "T")
                E_.wait(f"{tg}.e2")
                E_.op("transpose", psf(b, 7, 256, 1), b.t("SMALL")[0:1, 0:128],
                      b.t("IDF")[0:1, 0:1], inc=("T", 1))
                b.mark(f"{tg}.t2", "T")
            if eng == "vector":
                E_.wait(f"{tg}.t1")
                E_.op("tensor_copy", b.t("SMALLB")[:, 497:498], psf(b, 7, 0, 1), inc=("V", 1))
                b.mark(f"{tg}.e1", "V")
                E_.wait(f"{tg}.mmf")
                E_.op("tensor_copy", b.t("SMALL")[0:1, 0:128], psf(b, 7, 128, 128)[0:1, :],
                      inc=("V", 1))
                b.mark(f"{tg}.e2", "V")
                E_.wait(f"{tg}.t2")
                E_.op("tensor_copy", b.t("BIASA")[:, oc:oc + 1], psf(b, 7, 256, 1), inc=("V", 1))
                b.mark(f"{tg}.e3", "V")
        else:
            if eng == "tensor":
                if oc > 0:
                    E_.wait(f"wf.ba.{oc-1}.e3")
                E_.op("transpose", psf(b, 7, 256, 1), b.t("BR")[0:1, oc * 128:(oc + 1) * 128],
                      b.t("IDF")[0:1, 0:1], inc=("T", 1))
                b.mark(f"{tg}.t2", "T")
            if eng == "vector":
                E_.wait(f"{tg}.t2")
                E_.op("tensor_copy", b.t("BIASA")[:, oc:oc + 1], psf(b, 7, 256, 1), inc=("V", 1))
                b.mark(f"{tg}.e3", "V")

    if eng == "sync":
        E_.wait("wf.ba.23.e3")
        E_.dma(b.t("BR")[0:1, 0:D], b.t("b_proj"), inc=("DQ", 16))
        b.mark("wf.bproj_row", "DQ")
    for oc in range(KC):
        tg = f"wf.bp.{oc}"
        if eng == "tensor":
            E_.wait("wf.bproj_row")
            if oc > 0:
                E_.wait(f"wf.bp.{oc-1}.e")
            E_.op("transpose", psf(b, 7, 256, 1), b.t("BR")[0:1, oc * 128:(oc + 1) * 128],
                  b.t("IDF")[0:1, 0:1], inc=("T", 1))
            b.mark(f"{tg}.t", "T")
        if eng == "vector":
            E_.wait(f"{tg}.t")
            E_.op("tensor_copy", b.t("BIASP")[:, oc:oc + 1], psf(b, 7, 256, 1), inc=("V", 1))
            b.mark(f"{tg}.e", "V")


def phase_weights(b, eng, E_):
    NB = E * IC * 3

    if STAGE_N >= 2:
        if eng == "gpsimd":
            E_.wait("wf.tr.15")
            E_.dma(b.t("GSTG")[0:E, 0:D], b.t("router_w"), inc=("DG", 16))
            b.mark("pw.rw.cast", "DG")
            E_.wait("pw.rw.cast")
        if eng == "tensor":
            E_.wait("pw.rw.cast")
            E_.wait("pc.dma")
            E_.wait("wf.bp.7.e")
            E_.wait("a3.q3.h3.ydone")
            for ic in range(KC):
                E_.op("transpose", psbf(b, 6, ic * 8, 8),
                      b.t("GSTG")[0:E, ic * 128:(ic + 1) * 128], b.t("IDB")[0:E, 0:E],
                      inc=("T", 1) if ic == KC - 1 else None)
            b.mark("pw.rw.tr", "T")
        if eng == "vector":
            E_.wait("pw.rw.tr")
            for ic in range(KC):
                E_.op("tensor_copy", b.t("ROUTER")[:, ic * 8:(ic + 1) * 8],
                      psbf(b, 6, ic * 8, 8), inc=("V", 1) if ic == KC - 1 else None)
            b.mark("pw.rw.ev", "V")

    if STAGE_N >= 2:
        for n3 in range(NB):
            blk, kind_i = divmod(n3, 3)
            e, ic = divmod(blk, IC)
            kind = "gud"[kind_i]
            tag = f"pw.moe.{n3}"
            par = n3 % 2
            dsem = "DG0" if par == 0 else "DG1"
            if eng == "gpsimd":
                if n3 >= 2:
                    E_.wait(f"pw.moe.tr.{n3-2}")
                elif n3 == 0:
                    E_.wait("pw.rw.tr")
                if kind == "g":
                    E_.wait(f"pc2.g{e}")
                    E_.dma(b.t("GSTG")[:, par * 1024:(par + 1) * 1024],
                           b.t("gate_bf")[e * I_FF + ic * 128: e * I_FF + (ic + 1) * 128, :],
                           inc=(dsem, 16))
                elif kind == "u":
                    E_.wait(f"pc2.u{e}")
                    E_.dma(b.t("GSTG")[:, par * 1024:(par + 1) * 1024],
                           b.t("up_bf")[e * I_FF + ic * 128: e * I_FF + (ic + 1) * 128, :],
                           inc=(dsem, 16))
                else:
                    E_.wait(f"pc2.d{e}")
                    for hc in range(KC):
                        E_.dma(b.t("GSTG")[:, par * 1024 + hc * 128: par * 1024 + (hc + 1) * 128],
                               b.t("down_bf")[e * D + hc * 128: e * D + (hc + 1) * 128,
                                              ic * 128:(ic + 1) * 128],
                               inc=(dsem, 16))
                b.mark(f"{tag}.cast", dsem)
            if eng == "tensor":
                E_.wait(f"{tag}.cast")
                if n3 == 0:
                    E_.wait("pw.rw.ev")
                if n3 >= 2:
                    E_.wait(f"pw.moe.ev.{n3-2}")
                for cc in range(KC):
                    E_.op("transpose", psbf(b, 6 + par, (cc % 4) * 256, 128),
                          b.t("GSTG")[:, par * 1024 + cc * 128: par * 1024 + (cc + 1) * 128],
                          b.t("IDB")[:, :], inc=("T", 1) if cc == 3 or cc == KC - 1 else None)
                    if cc == 3:
                        b.mark(f"{tag}.tr4", "T")
                        E_.wait(f"{tag}.ev4")
                b.mark(f"pw.moe.tr.{n3}", "T")
            if eng == "vector":
                E_.wait(f"{tag}.tr4")
                if n3 >= 2:
                    E_.wait(f"pw.moe.out.{n3-2}")
                for cc in range(4):
                    E_.op("tensor_copy",
                          b.t("OSTG")[:, par * 1024 + cc * 128: par * 1024 + (cc + 1) * 128],
                          psbf(b, 6 + par, cc * 256, 128),
                          inc=("V", 1) if cc == 3 else None)
                b.mark(f"{tag}.ev4", "V")
                E_.wait(f"pw.moe.tr.{n3}")
                for cc in range(4, KC):
                    E_.op("tensor_copy",
                          b.t("OSTG")[:, par * 1024 + cc * 128: par * 1024 + (cc + 1) * 128],
                          psbf(b, 6 + par, (cc % 4) * 256, 128),
                          inc=("V", 1) if cc == KC - 1 else None)
                b.mark(f"pw.moe.ev.{n3}", "V")
            if eng == "scalar":
                E_.wait(f"pw.moe.ev.{n3}")
                dstt = {"g": "gate_t", "u": "up_t", "d": "down_t"}[kind]
                E_.dma(b.t(dstt)[blk * 128:(blk + 1) * 128, :],
                       b.t("OSTG")[:, par * 1024:(par + 1) * 1024],
                       inc=("DV", 16))
                b.mark(f"pw.moe.out.{n3}", "DV")

    mats = [("Wq", "wq_t"), ("Wk", "wk_t"), ("Wv", "wv_t"), ("Wo", "wo_t")]
    if STAGE_N >= 3:
        for step in range(4 * KC):
            mi, oc = divmod(step, KC)
            src, dst = mats[mi]
            tag = f"pw.ab.{step}"
            par = step % 2
            dsem = "DG0" if par == 0 else "DG1"
            if eng == "gpsimd":
                if step >= 2:
                    E_.wait(f"pw.ab.tr.{step-2}")
                else:
                    E_.wait(f"pw.moe.tr.{NB - 2 + step}")
                E_.dma(b.t("GSTG")[:, par * 1024:(par + 1) * 1024],
                       b.t(src)[oc * 128:(oc + 1) * 128, :], inc=(dsem, 16))
                b.mark(f"{tag}.cast", dsem)
            if eng == "tensor":
                E_.wait(f"{tag}.cast")
                if step < 2:
                    E_.wait(f"pw.moe.ev.{NB - 2 + step}")
                if step >= 2:
                    E_.wait(f"pw.ab.ev.{step-2}")
                for cc in range(KC):
                    E_.op("transpose", psbf(b, 6 + par, (cc % 4) * 256, 128),
                          b.t("GSTG")[:, par * 1024 + cc * 128: par * 1024 + (cc + 1) * 128],
                          b.t("IDB")[:, :], inc=("T", 1) if cc == 3 or cc == KC - 1 else None)
                    if cc == 3:
                        b.mark(f"{tag}.tr4", "T")
                        E_.wait(f"{tag}.ev4")
                b.mark(f"pw.ab.tr.{step}", "T")
            if eng == "vector":
                E_.wait(f"{tag}.tr4")
                if step >= 2:
                    E_.wait(f"pw.ab.out.{step-2}")
                else:
                    E_.wait(f"pw.moe.out.{NB - 2 + step}")
                for cc in range(4):
                    E_.op("tensor_copy",
                          b.t("OSTG")[:, par * 1024 + cc * 128: par * 1024 + (cc + 1) * 128],
                          psbf(b, 6 + par, cc * 256, 128),
                          inc=("V", 1) if cc == 3 else None)
                b.mark(f"{tag}.ev4", "V")
                E_.wait(f"pw.ab.tr.{step}")
                for cc in range(4, KC):
                    E_.op("tensor_copy",
                          b.t("OSTG")[:, par * 1024 + cc * 128: par * 1024 + (cc + 1) * 128],
                          psbf(b, 6 + par, (cc % 4) * 256, 128),
                          inc=("V", 1) if cc == KC - 1 else None)
                b.mark(f"pw.ab.ev.{step}", "V")
            if eng == "scalar":
                E_.wait(f"pw.ab.ev.{step}")
                for ic in range(KC):
                    E_.dma(b.t(dst)[ic * 128:(ic + 1) * 128, oc * 128:(oc + 1) * 128],
                           b.t("OSTG")[:, par * 1024 + ic * 128: par * 1024 + (ic + 1) * 128],
                           inc=("DV", 16))
                b.mark(f"pw.ab.out.{step}", "DV")

    if STAGE_N >= 3:
        if eng == "gpsimd":
            E_.wait("pw.ab.tr.31")
            E_.dma(b.t("GSTG")[0:CLEN, 0:D], b.t("q_c"), inc=("DG", 16))
            b.mark("pw.qc.cast", "DG")
            E_.wait("pw.qc.cast")
        if eng == "tensor":
            E_.wait("pw.qc.cast")
            E_.wait("pw.ab.ev.31")
            E_.wait("pw.ab.ev.30")
            for ic in range(KC):
                E_.op("transpose", psbf(b, 6, ic * 16, 16),
                      b.t("GSTG")[0:CLEN, ic * 128:(ic + 1) * 128], b.t("IDB")[0:CLEN, 0:CLEN],
                      inc=("T", 1) if ic == KC - 1 else None)
            b.mark("pw.qc.tr", "T")
        if eng == "vector":
            E_.wait("pw.qc.tr")
            E_.wait("pc.v")
            for hp in range(KC):
                E_.op("tensor_copy", b.t("QCB")[0:64, hp * 64: hp * 64 + 16],
                      psbf(b, 6, hp * 16, 16)[0:64, :])
                E_.op("tensor_copy", b.t("QCB")[64:128, hp * 64 + 32: hp * 64 + 48],
                      psbf(b, 6, hp * 16, 16)[64:128, :],
                      inc=("V", 1) if hp == KC - 1 else None)
            b.mark("pw.qc.ev", "V")

        if eng == "sync":
            E_.wait("wf.bproj_row")
            E_.wait("wf.bp.7.e")
            E_.wait("a3.q3.h3.recipb")
            E_.dma(b.t("BR")[0:1, 0:H], b.t("beta"), inc=("DQ", 16))
            b.mark("pw.beta.row", "DQ")
            E_.wait("pw.beta.row")
        if eng == "scalar":
            E_.wait("pw.beta.row")
            E_.op("activation", b.t("SMALL")[0:1, 0:H], b.t("BR")[0:1, 0:H], AF.Exp,
                  inc=("S", 1))
            b.mark("pw.beta.exp", "S")
        if eng == "vector":
            E_.wait("pw.beta.exp")
            E_.op("reciprocal", b.t("SMALL")[0:1, 32:32 + H], b.t("SMALL")[0:1, 0:H],
                  inc=("V", 1))
            b.mark("pw.beta.rec", "V")
            E_.wait("pw.beta.rec")
            E_.op("tensor_scalar_mul", b.t("SMALL")[0:1, 64:64 + H],
                  b.t("SMALL")[0:1, 32:32 + H], 0.125, inc=("V", 1))
            b.mark("pw.beta.scl", "V")
            E_.wait("pw.beta.scl")
            E_.op("tensor_copy", b.t("SMALLB")[0:1, 480:480 + H],
                  b.t("SMALL")[0:1, 64:64 + H], inc=("V", 1))
            b.mark("pw.beta.bf", "V")
        if eng == "tensor":
            E_.wait("pw.beta.bf")
            E_.wait("pw.qc.ev")
            for h in range(H):
                E_.mm(psf(b, 6, h, 1), b.t("ONESR")[0:1, :],
                      b.t("SMALLB")[0:1, 480 + h:480 + h + 1],
                      inc=("T", 1) if h == H - 1 else None)
            b.mark("pw.beta.bc", "T")
        if eng == "vector":
            E_.wait("pw.beta.bc")
            for h in range(H):
                E_.op("tensor_copy",
                      b.t("QSCALE")[(h % 2) * 64:(h % 2) * 64 + 64, h // 2: h // 2 + 1],
                      psf(b, 6, h, 1)[(h % 2) * 64:(h % 2) * 64 + 64, :],
                      inc=("V", 1) if h == H - 1 else None)
            b.mark("pw.beta.qs", "V")

        for mi, src in enumerate(["bq", "bk", "bv", "bo"]):
            if eng == "sync":
                if mi == 0:
                    E_.wait("pw.beta.qs")
                else:
                    E_.wait(f"pw.bb.{mi-1}.7.e")
                E_.dma(b.t("BR")[0:1, 0:D], b.t(src), inc=("DQ", 16))
                b.mark(f"pw.bb.row.{mi}", "DQ")
                E_.wait(f"pw.bb.row.{mi}")
            for oc in range(KC):
                tgb = f"pw.bb.{mi}.{oc}"
                if eng == "tensor":
                    E_.wait(f"pw.bb.row.{mi}")
                    if mi == 0 and oc == 0:
                        E_.wait("pw.beta.qs")
                    elif oc == 0:
                        E_.wait(f"pw.bb.{mi-1}.7.e")
                    else:
                        E_.wait(f"pw.bb.{mi}.{oc-1}.e")
                    E_.op("transpose", psf(b, 6, 0, 1),
                          b.t("BR")[0:1, oc * 128:(oc + 1) * 128], b.t("IDF")[0:1, 0:1],
                          inc=("T", 1))
                    b.mark(f"{tgb}.t", "T")
                if eng == "vector":
                    E_.wait(f"{tgb}.t")
                    E_.op("tensor_copy", b.t("BIASB")[:, mi * 8 + oc: mi * 8 + oc + 1],
                          psf(b, 6, 0, 1), inc=("V", 1))
                    b.mark(f"{tgb}.e", "V")


def attn_a_seq(b, eng, E_, s):
    NT = T // 128

    for tt in range(NT):
        tag = f"a{s}.ln{tt}"
        par = tt % 2
        xsem = "DX0" if par == 0 else "DX1"
        if eng == "sync":
            if tt >= 2:
                E_.wait(f"a{s}.ln{tt-2}.nst")
            elif s > 0:
                E_.wait(f"a{s-1}.pj.done")
            E_.dma(b.t("AT")[:, 0:1024] if par == 0 else b.t("AT")[:, 1024:2048],
                   b.t("x")[s * T + tt * 128: s * T + (tt + 1) * 128, :], inc=(xsem, 16))
            b.mark(f"{tag}.x", xsem)
        ATh = b.t("AT")[:, 0:1024] if par == 0 else b.t("AT")[:, 1024:2048]
        if eng == "scalar":
            E_.wait(f"{tag}.x")
            if tt >= 1:
                E_.wait(f"a{s}.ln{tt-1}.var")
            elif s > 0:
                E_.wait(f"a{s-1}.pj.n3.t3.tr")
                E_.wait(f"a{s-1}.pj.n3.pev")
            E_.op("activation", b.t("AF1")[:, 0:512], ATh[:, 0:512],
                  AF.Square, accum_out=b.t("STAT")[:, 0:1])
            E_.op("activation", b.t("AF1")[:, 512:1024], ATh[:, 512:1024],
                  AF.Square, accum_out=b.t("STAT")[:, 1:2], inc=("S", 1))
            b.mark(f"{tag}.sq", "S")
        if eng == "vector":
            E_.wait(f"{tag}.x")
            if tt >= 1:
                E_.wait(f"a{s}.ln{tt-1}.rstd")
            E_.op("reduce_sum", b.t("STAT")[:, 2:3], ATh, axis=AX, inc=("V", 1))
            b.mark(f"{tag}.sum", "V")
            E_.wait(f"{tag}.sq")
            E_.op("tensor_add", b.t("STAT")[:, 3:4], b.t("STAT")[:, 0:1], b.t("STAT")[:, 1:2])
            E_.op("tensor_scalar_mul", b.t("STAT")[:, 4:5], b.t("STAT")[:, 2:3], 1.0 / D)
            E_.op("tensor_scalar_mul", b.t("STAT")[:, 5:6], b.t("STAT")[:, 3:4], 1.0 / D)
            E_.op("tensor_mul", b.t("STAT")[:, 6:7], b.t("STAT")[:, 4:5], b.t("STAT")[:, 4:5])
            E_.op("tensor_sub", b.t("STAT")[:, 7:8], b.t("STAT")[:, 5:6], b.t("STAT")[:, 6:7],
                  inc=("V", 1))
            b.mark(f"{tag}.var", "V")
        if eng == "scalar":
            E_.wait(f"{tag}.var")
            E_.op("activation", b.t("STAT")[:, 8:9], b.t("STAT")[:, 7:8], AF.Ln, bias=1e-5)
            E_.op("activation", b.t("STAT")[:, 9:10], b.t("STAT")[:, 8:9], AF.Exp, scale=-0.5,
                  inc=("S", 1))
            b.mark(f"{tag}.rstd", "S")
        if eng == "vector":
            E_.wait(f"{tag}.rstd")
            if tt >= 1:
                E_.wait(f"a{s}.ln{tt-1}.ntr")
            E_.op("tensor_scalar", b.t("AXS")[:, 0:1024], ATh,
                  b.t("STAT")[:, 4:5], b.t("STAT")[:, 9:10], ALU.subtract, ALU.mult,
                  inc=("V", 1))
            b.mark(f"{tag}.nst", "V")
        if eng == "tensor":
            E_.wait(f"{tag}.nst")
            if s == 0 and tt == 0:
                E_.wait("wf.bp.7.e")
            if tt >= 1:
                E_.wait(f"a{s}.ln{tt-1}.nev")
            elif s > 0:
                E_.wait(f"a{s-1}.pj.done")
            for kc2 in range(KC):
                E_.op("transpose",
                      psbf(b, 6, (kc2 % 2) * 128 + (kc2 // 2) * 512, 128),
                      b.t("AXS")[:, kc2 * 128:(kc2 + 1) * 128], b.t("IDB")[:, :],
                      inc=("T", 1) if kc2 == KC - 1 else None)
            b.mark(f"{tag}.ntr", "T")
        if eng == "vector":
            E_.wait(f"{tag}.ntr")
            if tt >= 1:
                E_.wait(f"a{s}.ln{tt-1}.nout")
            for kc2 in range(KC):
                E_.op("tensor_copy", b.t("AXS")[:, 1024 + kc2 * 128: 1024 + (kc2 + 1) * 128],
                      psbf(b, 6, (kc2 % 2) * 128 + (kc2 // 2) * 512, 128),
                      inc=("V", 1) if kc2 == KC - 1 else None)
            b.mark(f"{tag}.nev", "V")
        if eng == "sync":
            E_.wait(f"{tag}.nev")
            for kc2 in range(KC):
                E_.dma(b.t("xln_spill")[kc2 * 128:(kc2 + 1) * 128, tt * 128:(tt + 1) * 128],
                       b.t("AXS")[:, 1024 + kc2 * 128: 1024 + (kc2 + 1) * 128], inc=("DQ", 16))
            b.mark(f"{tag}.nout", "DQ")

    for qr in range(4):
        attn_a_quarter(b, eng, E_, s, qr)

    attn_a_proj(b, eng, E_, s)


def attn_a_quarter(b, eng, E_, s, qr):
    tq = f"a{s}.q{qr}"

    for n in range(4):
        tag = f"{tq}.n{n}"
        for kc in range(KC):
            par = kc % 2
            xsem = "DX0" if par == 0 else "DX1"
            if eng == "sync":
                pkc, pn = kc - 2, n
                if pkc < 0:
                    pkc, pn = kc + 6, n - 1
                if pn >= 0:
                    E_.wait(f"{tq}.n{pn}.mm{pkc}")
                elif qr > 0:
                    E_.wait(f"a{s}.q{qr-1}.n3.mm{pkc}")
                else:
                    E_.wait(f"a{s}.ln15.ntr")
                    E_.wait(f"a{s}.ln15.nout")
                E_.dma(b.t("AXS")[:, par * 512: par * 512 + 512],
                       b.t("xln_spill")[kc * 128:(kc + 1) * 128, n * 512:(n + 1) * 512],
                       inc=(xsem, 16))
                b.mark(f"{tag}.xd{kc}", xsem)
            if eng == "tensor":
                E_.wait(f"{tag}.xd{kc}")
                if kc == 0:
                    E_.wait("wf.vcols")
                    E_.wait("wf.ev.15")
                    E_.wait(f"a{s}.ln15.nev")
                    if n > 0:
                        E_.wait(f"{tq}.n{n-1}.ev")
                        E_.wait(f"{tq}.n{n-1}.evv")
                        pass
                    elif qr > 0:
                        E_.wait(f"a{s}.q{qr-1}.done")
                    elif s > 0:
                        E_.wait(f"a{s-1}.pj.done")
                st, sp = (kc == 0), (kc == KC - 1)
                AWt = b.t("AW")
                AXSh = b.t("AXS")[:, par * 512: par * 512 + 512]
                for j in range(2):
                    E_.mm(psf(b, j), AWt[:, kc * 3072 + qr * 256 + j * 128: kc * 3072 + qr * 256 + (j + 1) * 128],
                          AXSh, start=st, stop=sp)
                for j in range(2):
                    E_.mm(psf(b, 2 + j), AWt[:, kc * 3072 + 1024 + qr * 256 + j * 128: kc * 3072 + 1024 + qr * 256 + (j + 1) * 128],
                          AXSh, start=st, stop=sp)
                for v4 in range(4):
                    E_.mm(psf(b, 4 + v4, 0, 256),
                          AXSh[:, v4 * 128:(v4 + 1) * 128],
                          AWt[:, kc * 3072 + 2048 + qr * 256: kc * 3072 + 2048 + (qr + 1) * 256],
                          start=st, stop=False,
                          inc=("T", 1) if v4 == 3 else None)
                b.mark(f"{tag}.mm{kc}", "T")
        if eng == "tensor":
            E_.wait("wf.bv")
            for v4 in range(4):
                E_.mm(psf(b, 4 + v4, 0, 256),
                      b.t("ONESR")[0:1, 0:128], b.t("ROWB2")[0:1, qr * 256:(qr + 1) * 256],
                      start=False, stop=True,
                      inc=("T", 1) if v4 == 3 else None)
            b.mark(f"{tag}.vb", "T")
        if eng == "vector":
            E_.wait(f"{tag}.mm7")
            for j in range(2):
                E_.op("tensor_scalar_add",
                      b.t("AQ")[:, j * 2048 + n * 512: j * 2048 + n * 512 + 512],
                      psf(b, j), b.t("BIASA")[:, qr * 2 + j: qr * 2 + j + 1])
            for j in range(2):
                E_.op("tensor_scalar_add",
                      b.t("AQ")[:, 4096 + j * 2048 + n * 512: 4096 + j * 2048 + n * 512 + 512],
                      psf(b, 2 + j), b.t("BIASA")[:, 8 + qr * 2 + j: 8 + qr * 2 + j + 1],
                      inc=("V", 1) if j == 1 else None)
            b.mark(f"{tag}.ev", "V")
            E_.wait(f"{tag}.vb")
            for v4 in range(4):
                ti = n * 4 + v4
                E_.op("tensor_copy",
                      b.t("AQ")[:, 8192 + ti * 264: 8192 + ti * 264 + 264]
                      .rearrange("p (h e) -> p h e", h=4)[:, :, 0:64],
                      psf(b, 4 + v4, 0, 256)
                      .rearrange("p (h d) -> p h d", h=4),
                      inc=("V", 1) if v4 == 3 else None)
            b.mark(f"{tag}.evv", "V")
    if DBG and s == 0 and qr == 0 and eng == "gpsimd":
        E_.wait(f"a0.q0.n3.evv")
        E_.dma(b.t("dbg_aq"), b.t("AQ")[:, :], inc=("DG", 16))
        E_.dma(b.t("dbg_aw"), b.t("AW")[:, :], inc=("DG", 16))
        b.mark("dbg3.done", "DG")
        E_.wait("dbg3.done")
    if DBG and s == 0 and qr == 0 and eng == "vector":
        E_.wait("dbg3.done")
    for hh in range(4):
        attn_a_head(b, eng, E_, s, qr, hh)

    if DBG and s == 0 and qr == 0 and eng == "gpsimd":
        E_.wait(f"a0.q0.h3.ydone")
        E_.dma(b.t("dbg_ay"), b.t("AY")[:, :], inc=("DG", 16))
        b.mark("dbg4.done", "DG")
        E_.wait("dbg4.done")
    if eng == "vector":
        b.mark(f"{tq}.done", "V")
    if eng == "tensor":
        b.mark(f"{tq}.tdone", "T")


def attn_a_head(b, eng, E_, s, qr, hh):
    tg = f"a{s}.q{qr}.h{hh}"
    j, r0 = hh // 2, (hh % 2) * 64
    qcol0, kcol0 = j * 2048, 4096 + j * 2048
    NKT = T // 128

    for wi, (which, col0) in enumerate((("q", qcol0), ("k", kcol0))):
        for n in range(4):
            tn = f"{tg}.nr{which}{n}"
            if eng == "vector":
                E_.wait(f"a{s}.q{qr}.n3.evv")
                E_.wait("pc.v")
                prev = (wi * 4 + n) - 1
                if prev >= 0:
                    pw, pn = ("q", prev) if prev < 4 else ("k", prev - 4)
                    E_.wait(f"{tg}.nr{pw}{pn}.mm")
                elif hh > 0:
                    E_.wait(f"a{s}.q{qr}.h{hh-1}.nrk3.mm")
                elif qr > 0:
                    E_.wait(f"a{s}.q{qr-1}.h3.nrk3.mm")
                elif s > 0:
                    E_.wait(f"a{s-1}.q3.h3.nrk3.mm")
                E_.op("tensor_mul", b.t("SMALLB")[64:128, 0:512],
                      b.t("AQ")[r0:r0 + 64, col0 + n * 512: col0 + (n + 1) * 512],
                      b.t("AQ")[r0:r0 + 64, col0 + n * 512: col0 + (n + 1) * 512],
                      inc=("V", 1))
                b.mark(f"{tn}.sq", "V")
            if eng == "tensor":
                E_.wait(f"{tn}.sq")
                if n > 0:
                    E_.wait(f"{tg}.nr{which}{n-1}.max")
                if n == 0 and wi == 0:
                    if hh > 0:
                        E_.wait(f"a{s}.q{qr}.h{hh-1}.nrmax")
                    elif qr > 0:
                        E_.wait(f"a{s}.q{qr-1}.h3.nrmax")
                    elif s > 0:
                        E_.wait(f"a{s-1}.q3.h3.nrmax")
                E_.mm(psf(b, 2 + wi, 0, 512)[0:1, :],
                      b.t("SMALLB")[64:128, 496:497],
                      b.t("SMALLB")[64:128, 0:512],
                      inc=("T", 1))
                b.mark(f"{tn}.mm", "T")
            if eng == "vector":
                E_.wait(f"{tn}.mm")
                E_.op("reduce_max", b.t("STAT")[0:1, wi * 4 + n: wi * 4 + n + 1],
                      psf(b, 2 + wi, 0, 512)[0:1, :], axis=AX, inc=("V", 1))
                b.mark(f"{tn}.max", "V")
    if eng == "vector":
        E_.wait(f"{tg}.nrk3.max")
        E_.op("reduce_max", b.t("STAT")[0:1, 8:9], b.t("STAT")[0:1, 0:4], axis=AX,
              inc=("V", 1))
        b.mark(f"{tg}.mx1", "V")
        E_.wait(f"{tg}.mx1")
        E_.op("reduce_max", b.t("STAT")[0:1, 9:10], b.t("STAT")[0:1, 4:8], axis=AX,
              inc=("V", 1))
        b.mark(f"{tg}.mx2", "V")
        E_.wait(f"{tg}.mx2")
        E_.op("tensor_mul", b.t("STAT")[0:1, 10:11], b.t("STAT")[0:1, 8:9],
              b.t("STAT")[0:1, 9:10], inc=("V", 1))
        b.mark(f"{tg}.nrmax", "V")
    if eng == "scalar":
        E_.wait(f"{tg}.nrmax")
        E_.op("activation", b.t("STAT")[0:1, 11:12], b.t("STAT")[0:1, 10:11], AF.Ln,
              inc=("S", 1))
        b.mark(f"{tg}.m0", "S")
        E_.wait(f"{tg}.m0")
        E_.op("activation", b.t("STAT")[0:1, 12:13], b.t("STAT")[0:1, 11:12], AF.Exp,
              scale=0.5, inc=("S", 1))
        b.mark(f"{tg}.m", "S")
    if eng == "vector":
        E_.wait(f"{tg}.m")
        E_.op("tensor_scalar_mul", b.t("STAT")[0:1, 13:14], b.t("STAT")[0:1, 12:13], -0.125,
              inc=("V", 1))
        b.mark(f"{tg}.mb0", "V")
        E_.wait(f"{tg}.mb0")
        E_.op("tensor_copy", b.t("ROWB")[0:1, 2304:2305], b.t("STAT")[0:1, 13:14],
              inc=("V", 1))
        b.mark(f"{tg}.mb", "V")
    if eng == "tensor":
        E_.wait(f"{tg}.mb")
        E_.mm(psf(b, 3, 0, 1), b.t("ONESR")[0:1, 0:128], b.t("ROWB")[0:1, 2304:2305],
              inc=("T", 1))
        b.mark(f"{tg}.mbb", "T")
    if eng == "vector":
        E_.wait(f"{tg}.mbb")
        if hh > 0:
            E_.wait(f"a{s}.q{qr}.h{hh-1}.expall")
        elif qr > 0:
            E_.wait(f"a{s}.q{qr-1}.h3.expall")
        elif s > 0:
            E_.wait(f"a{s-1}.q3.h3.expall")
        E_.op("tensor_copy", b.t("STAT")[:, 14:15], psf(b, 3, 0, 1), inc=("V", 1))
        b.mark(f"{tg}.mbias", "V")

    def chunks(kt):
        out, a = [], kt * 128
        while a < T:
            w = min(512 - (a % 512), T - a)
            out.append((a, w))
            a += w
        return out

    cof = {}
    chname = []
    for kt_ in range(NKT):
        for cix_, _ in enumerate(chunks(kt_)):
            cof[(kt_, cix_)] = len(chname)
            chname.append(f"{tg}.kt{kt_}.c{cix_}")

    def emit_qk(kt):
        for cix, (a, w) in enumerate(chunks(kt)):
            tc = f"{tg}.kt{kt}.c{cix}"
            cg = cof[(kt, cix)]
            if cg >= 4:
                E_.wait(f"{chname[cg-4]}.exp")
            if kt == 0 and cix == 0:
                E_.wait(f"{tg}.mbias")
                if hh > 0:
                    E_.wait(f"a{s}.q{qr}.h{hh-1}.ydone")
                elif qr > 0:
                    E_.wait(f"a{s}.q{qr-1}.h3.ydone")
                elif s > 0:
                    E_.wait(f"a{s-1}.q3.h3.ydone")
            E_.mm(psf(b, cg % 4, 0, w),
                  b.t("AQ")[r0:r0 + 64, kcol0 + kt * 128: kcol0 + (kt + 1) * 128],
                  b.t("AQ")[r0:r0 + 64, qcol0 + a: qcol0 + a + w],
                  inc=("T", 1))
            b.mark(f"{tc}.mm", "T")

    def emit_av(kt):
        tg2 = f"{tg}.kt{kt}"
        spar = kt % 2
        ch = chunks(kt)
        for cix, (a, w) in enumerate(ch):
            E_.wait(f"{tg2}.c{cix}.exp")
            if cix == 0:
                E_.wait(f"{tg2}.mask")
            ybank = 4 + a // 512
            E_.mm(psf(b, ybank, a % 512, w)[0:65, :],
                  b.t("AQ")[:, 8192 + kt * 264 + hh * 66: 8192 + kt * 264 + hh * 66 + 65],
                  b.t("APT")[:, spar * 2048 + a - kt * 128: spar * 2048 + a - kt * 128 + w],
                  start=(kt == 0), stop=(kt == min(NKT - 1, (a // 512) * 4 + 3)),
                  skip_group_check=True,
                  inc=("T", 1) if cix == len(ch) - 1 else None)
        b.mark(f"{tg2}.avdone", "T")

    if eng == "tensor":
        for kt in range(NKT):
            emit_qk(kt)
            if kt >= 1:
                emit_av(kt - 1)
        emit_av(NKT - 1)
    if eng == "scalar":
        for kt in range(NKT):
            tg2 = f"{tg}.kt{kt}"
            spar = kt % 2
            for cix, (a, w) in enumerate(chunks(kt)):
                tc = f"{tg2}.c{cix}"
                cg = cof[(kt, cix)]
                E_.wait(f"{tc}.mm")
                if kt >= 2 and cix == 0:
                    E_.wait(f"{tg}.kt{kt-2}.avdone")
                E_.op("activation",
                      b.t("APT")[:, spar * 2048 + a - kt * 128: spar * 2048 + a - kt * 128 + w],
                      psf(b, cg % 4, 0, w), AF.Exp,
                      bias=b.t("STAT")[:, 14:15], scale=0.125,
                      inc=("S", 1))
                b.mark(f"{tc}.exp", "S")
        b.mark(f"{tg}.expall", "S")
    if eng == "vector":
        for kt in range(NKT):
            tg2 = f"{tg}.kt{kt}"
            spar = kt % 2
            E_.wait(f"{tg2}.c0.exp")
            E_.op("tensor_mul", b.t("APT")[:, spar * 2048: spar * 2048 + 128],
                  b.t("APT")[:, spar * 2048: spar * 2048 + 128], b.t("CAUSAL")[:, :],
                  inc=("V", 1))
            b.mark(f"{tg2}.mask", "V")

    if eng == "vector":
        E_.wait(f"{tg}.kt15.avdone")
        E_.op("reciprocal", b.t("BR")[0:1, 0:2048],
              b.t("PS")[64:65, 2048:4096], inc=("V", 1))
        b.mark(f"{tg}.recip", "V")
        E_.wait(f"{tg}.recip")
        E_.op("tensor_copy", b.t("ROWB")[0:1, 0:2048], b.t("BR")[0:1, 0:2048], inc=("V", 1))
        b.mark(f"{tg}.recipb", "V")
    hglob = qr * 4 + hh
    ych, yr0 = hglob // 2, (hglob % 2) * 64
    for n in range(4):
        if eng == "tensor":
            E_.wait(f"{tg}.recipb")
            if n == 0:
                E_.wait(f"{tg}.expall")
            else:
                E_.wait(f"{tg}.ye{n-1}")
            E_.mm(psf(b, n % 2, 0, 512)[0:64, :], b.t("ONESR")[0:1, 0:64],
                  b.t("ROWB")[0:1, n * 512:(n + 1) * 512],
                  inc=("T", 1))
            b.mark(f"{tg}.db{n}", "T")
        if eng == "vector":
            E_.wait(f"{tg}.db{n}")
            E_.op("tensor_copy", b.t("SMALLB")[0:64, 0:512], psf(b, n % 2, 0, 512)[0:64, :],
                  inc=("V", 1))
            b.mark(f"{tg}.dbe{n}", "V")
            E_.wait(f"{tg}.dbe{n}")
            E_.op("scalar_tensor_tensor",
                  b.t("AY")[yr0:yr0 + 64, ych * 2048 + n * 512: ych * 2048 + (n + 1) * 512],
                  b.t("PS")[0:64, 2048 + n * 512: 2048 + (n + 1) * 512],
                  1.0, b.t("SMALLB")[0:64, 0:512], ALU.mult, ALU.mult,
                  inc=("V", 1))
            b.mark(f"{tg}.ye{n}", "V")
    if eng == "vector":
        b.mark(f"{tg}.ydone", "V")


def attn_a_proj(b, eng, E_, s):
    tg = f"a{s}.pj"
    for n in range(4):
        tn = f"{tg}.n{n}"
        for kc in range(KC):
            par = kc % 2
            wsem = "DW0" if par == 0 else "DW1"
            if eng == "sync":
                pkc, pn = kc - 2, n
                if pkc < 0:
                    pkc, pn = kc + 6, n - 1
                if pn >= 0:
                    E_.wait(f"{tg}.n{pn}.pmm{pkc}")
                elif s > 0:
                    E_.wait(f"a{s-1}.pj.n3.pmm{pkc}")
                else:
                    E_.wait(f"wf.mm.{15}")
                    E_.wait("wf.castp")
                E_.dma(b.t("WSTG")[:, par * 1024:(par + 1) * 1024],
                       b.t("wproj_bf")[kc * 128:(kc + 1) * 128, :], inc=(wsem, 16))
                b.mark(f"{tn}.wd{kc}", wsem)
            if eng == "tensor":
                E_.wait(f"{tn}.wd{kc}")
                if kc == 0:
                    E_.wait(f"a{s}.q3.done")
                    if n > 0:
                        E_.wait(f"{tg}.n{n-1}.t3.stt")
                st, sp = (kc == 0), (kc == KC - 1)
                for jc in range(KC):
                    E_.mm(psf(b, jc),
                          b.t("WSTG")[:, par * 1024 + jc * 128: par * 1024 + (jc + 1) * 128],
                          b.t("AY")[:, kc * 2048 + n * 512: kc * 2048 + (n + 1) * 512],
                          start=st, stop=sp,
                          inc=("T", 1) if jc == KC - 1 else None)
                b.mark(f"{tn}.pmm{kc}", "T")
        if eng == "vector":
            E_.wait(f"{tn}.pmm7")
            for jc in range(KC):
                E_.op("tensor_scalar_add", b.t("AF1")[:, jc * 512:(jc + 1) * 512],
                      psf(b, jc), b.t("BIASP")[:, jc:jc + 1],
                      inc=("V", 1) if jc == KC - 1 else None)
            b.mark(f"{tn}.pev", "V")
        for t4 in range(4):
            tt = n * 4 + t4
            tb = f"{tn}.t{t4}"
            tpar = t4 % 2
            if eng == "sync":
                xsem = "DX0" if tpar == 0 else "DX1"
                if tt >= 2:
                    E_.wait(f"{tg}.n{(tt-2)//4}.t{(tt-2)%4}.stt")
                E_.dma(b.t("AT")[:, tpar * 1024:(tpar + 1) * 1024],
                       b.t("x")[s * T + tt * 128: s * T + (tt + 1) * 128, :],
                       inc=(xsem, 16))
                b.mark(f"{tb}.x", xsem)
            if eng == "tensor":
                E_.wait(f"{tn}.pev") if t4 == 0 else None
                if tt >= 2:
                    E_.wait(f"{tg}.n{(tt-2)//4}.t{(tt-2)%4}.stt")
                for jc in range(KC):
                    E_.op("transpose",
                          psf(b, tpar * 2 + jc // 4, (jc % 4) * 128, 128),
                          b.t("AF1")[:, jc * 512 + t4 * 128: jc * 512 + t4 * 128 + 128],
                          b.t("IDF")[:, :],
                          inc=("T", 1) if jc == KC - 1 else None)
                b.mark(f"{tb}.tr", "T")
            if eng == "vector":
                E_.wait(f"{tb}.tr")
                E_.wait(f"{tb}.x")
                if tt >= 2:
                    E_.wait(f"{tg}.n{(tt-2)//4}.t{(tt-2)%4}.st")
                elif s > 0:
                    E_.wait(f"a{s-1}.pj.n3.t{2 + t4}.st")
                E_.op("scalar_tensor_tensor",
                      b.t("AF1")[:, 4096 + tpar * 1024: 4096 + (tpar + 1) * 1024],
                      b.t("PS")[:, tpar * 1024:(tpar + 1) * 1024],
                      1.0, b.t("AT")[:, tpar * 1024:(tpar + 1) * 1024],
                      ALU.mult, ALU.add, inc=("V", 1))
                b.mark(f"{tb}.stt", "V")
            if eng == "sync":
                E_.wait(f"{tb}.stt")
                ssem = "DS0" if tpar == 0 else "DS1"
                E_.dma(b.t("x1_spill")[s * T + tt * 128: s * T + (tt + 1) * 128, :],
                       b.t("AF1")[:, 4096 + tpar * 1024: 4096 + (tpar + 1) * 1024],
                       inc=(ssem, 16))
                b.mark(f"{tb}.st", ssem)
    if eng == "sync":
        b.mark(f"{tg}.sdone0", "DS0")
        b.mark(f"{tg}.sdone1", "DS1")
    if eng == "vector":
        b.mark(f"{tg}.vdone", "V")
    if eng == "tensor":
        b.mark(f"{tg}.done", "T")




def phase_ffn(b, eng, E_, app):
    for sg in range(NSG):
        ffn_sgroup(b, eng, E_, app, sg)


def ffn_sgroup(b, eng, E_, app, sg):
    pre = f"f{app}.{sg}"
    prev = f"f{app}.{sg-1}"
    src = "x1_spill" if app == 1 else "x3_local"
    dst = "x2_spill" if app == 1 else "out"
    t0 = sg * SG
    first = (sg == 0)
    pva = "a3.pj.vdone" if app == 1 else "b.done.v"
    psrc = ["a3.pj.sdone0", "a3.pj.sdone1"] if app == 1 else ["b.sdone0", "b.sdone1"]
    NTT = SG // 128

    if app == 1 and STAGE_N >= 3 and sg in (3, 5, 7) and eng == "gpsimd":
        gq = (sg - 3) // 2
        E_.wait(f"f1.{2 * gq + 1}.sd0")
        E_.wait(f"f1.{2 * gq + 1}.sd1")
        if b.real:
            inst = E_.e.collective_compute(
                "AllGather", ALU.bypass,
                replica_groups=[list(range(NCORES))],
                ins=[b.ctx["x2_spill"][gq * 2048:(gq + 1) * 2048, :].opt()],
                outs=[b.ctx[f"x2_ag{gq}"].opt()],
            )
            inst.then_inc(b.sem["C"], 1)
        b.bump("C", 1)
        b.mark(f"rs.g{gq}", "C")

    for tt in range(NTT):
        tag = f"{pre}.ln{tt}"
        par = tt % 2
        xsem = "DX0" if par == 0 else "DX1"
        ATh = b.t("AF1")[:, tt * 1024:(tt + 1) * 1024]
        if eng == "sync":
            if not first:
                E_.wait(f"{prev}.fe{tt}.st")
            else:
                E_.wait(pva)
                for cp_ in psrc:
                    E_.wait(cp_)
                E_.wait("a3.pj.done" if app == 1 else "b.done.t")
            E_.dma(ATh, b.t(src)[t0 + tt * 128: t0 + (tt + 1) * 128, :], inc=(xsem, 16))
            b.mark(f"{tag}.x", xsem)
        if eng == "scalar":
            E_.wait(f"{tag}.x")
            if tt >= 1:
                E_.wait(f"{pre}.ln{tt-1}.var")
            elif first and app == 1 and STAGE_N >= 2:
                E_.wait("pw.qc.tr" if STAGE_N >= 3 else f"pw.moe.tr.{E*IC*3-1}")
                if STAGE_N >= 3:
                    E_.wait("pw.qc.cast")
                else:
                    E_.wait(f"pw.moe.{E*IC*3-1}.cast")
                    E_.wait(f"pw.moe.{E*IC*3-2}.cast")
            elif not first:
                E_.wait(f"{prev}.e{E-1}.mm{IC-1}")
            gs32 = b.t("GSTG").bitcast(F32)
            E_.op("activation", gs32[:, 0:512], ATh[:, 0:512],
                  AF.Square, accum_out=b.t("STAT")[:, 0:1])
            E_.op("activation", gs32[:, 512:1024], ATh[:, 512:1024],
                  AF.Square, accum_out=b.t("STAT")[:, 1:2], inc=("S", 1))
            b.mark(f"{tag}.sq", "S")
        if eng == "vector":
            E_.wait(f"{tag}.x")
            if tt >= 1:
                E_.wait(f"{pre}.ln{tt-1}.rstd")
            elif first:
                E_.wait(pva)
            E_.op("reduce_sum", b.t("STAT")[:, 2:3], ATh, axis=AX, inc=("V", 1))
            b.mark(f"{tag}.sum", "V")
            E_.wait(f"{tag}.sq")
            E_.op("tensor_add", b.t("STAT")[:, 3:4], b.t("STAT")[:, 0:1], b.t("STAT")[:, 1:2])
            E_.op("tensor_scalar_mul", b.t("STAT")[:, 4:5], b.t("STAT")[:, 2:3], 1.0 / D)
            E_.op("tensor_scalar_mul", b.t("STAT")[:, 5:6], b.t("STAT")[:, 3:4], 1.0 / D)
            E_.op("tensor_mul", b.t("STAT")[:, 6:7], b.t("STAT")[:, 4:5], b.t("STAT")[:, 4:5])
            E_.op("tensor_sub", b.t("STAT")[:, 7:8], b.t("STAT")[:, 5:6], b.t("STAT")[:, 6:7],
                  inc=("V", 1))
            b.mark(f"{tag}.var", "V")
        if eng == "scalar":
            E_.wait(f"{tag}.var")
            E_.op("activation", b.t("STAT")[:, 8:9], b.t("STAT")[:, 7:8], AF.Ln, bias=1e-5)
            E_.op("activation", b.t("STAT")[:, 9:10], b.t("STAT")[:, 8:9], AF.Exp, scale=-0.5,
                  inc=("S", 1))
            b.mark(f"{tag}.rstd", "S")
        if eng == "vector":
            E_.wait(f"{tag}.rstd")
            if tt >= 1:
                E_.wait(f"{pre}.ln{tt-1}.lntr")
            elif first and app == 1:
                E_.wait("a3.q3.n3.mm7")
            elif first and app == 2:
                E_.wait("b.done.t")
            E_.op("tensor_scalar", b.t("AXS")[:, 0:1024], ATh,
                  b.t("STAT")[:, 4:5], b.t("STAT")[:, 9:10], ALU.subtract, ALU.mult,
                  inc=("V", 1))
            b.mark(f"{tag}.nst", "V")
        if eng == "tensor":
            E_.wait(f"{tag}.nst")
            if tt >= 1:
                E_.wait(f"{pre}.ln{tt-1}.lnev")
            elif not first:
                E_.wait(f"{prev}.lnevlast")
            for kc2 in range(KC):
                E_.op("transpose",
                      psbf(b, 6, (kc2 % 2) * 128 + (kc2 // 2) * 512, 128),
                      b.t("AXS")[:, kc2 * 128:(kc2 + 1) * 128], b.t("IDB")[:, :],
                      inc=("T", 1) if kc2 == KC - 1 else None)
            b.mark(f"{tag}.lntr", "T")
        if eng == "vector":
            E_.wait(f"{tag}.lntr")
            for kc2 in range(KC):
                E_.op("tensor_copy", b.t("AY")[:, kc2 * 1024 + tt * 128: kc2 * 1024 + (tt + 1) * 128],
                      psbf(b, 6, (kc2 % 2) * 128 + (kc2 // 2) * 512, 128),
                      inc=("V", 1) if kc2 == KC - 1 else None)
            b.mark(f"{tag}.lnev", "V")
    if eng == "vector":
        b.mark(f"{pre}.lnevlast", "V")

    for rt in range(NTT):
        tag = f"{pre}.rt{rt}"
        if eng == "tensor":
            E_.wait(f"{pre}.ln{rt}.lnev")
            if rt >= 1:
                E_.wait(f"{pre}.rt{rt-1}.exp")
            elif first and app == 1:
                E_.wait("pw.rw.ev")
            for kc in range(KC):
                E_.mm(psf(b, 5, 0, 8),
                      b.t("AY")[:, kc * 1024 + rt * 128: kc * 1024 + (rt + 1) * 128],
                      b.t("ROUTER")[:, kc * 8:(kc + 1) * 8],
                      start=(kc == 0), stop=(kc == KC - 1),
                      inc=("T", 1) if kc == KC - 1 else None)
            b.mark(f"{tag}.lg", "T")
        if eng == "vector":
            E_.wait(f"{tag}.lg")
            E_.op("reduce_max", b.t("STAT")[:, 10:11], psf(b, 5, 0, 8), axis=AX, negate=True,
                  inc=("V", 1))
            b.mark(f"{tag}.m1", "V")
        if eng == "scalar":
            E_.wait(f"{tag}.m1")
            E_.op("activation", b.t("SMALL")[:, 32:40], psf(b, 5, 0, 8), AF.Exp,
                  bias=b.t("STAT")[:, 10:11], accum_out=b.t("STAT")[:, 11:12], inc=("S", 1))
            b.mark(f"{tag}.exp", "S")
        if eng == "vector":
            E_.wait(f"{tag}.exp")
            E_.op("max", b.t("SMALL")[:, 40:48], b.t("SMALL")[:, 32:40])
            E_.op("tensor_scalar", b.t("SMALL")[:, 24:32], b.t("SMALL")[:, 32:40],
                  b.t("SMALL")[:, 41:42], None, ALU.is_ge)
            E_.op("tensor_mul", b.t("SMALL")[:, 16:24], b.t("SMALL")[:, 32:40],
                  b.t("SMALL")[:, 24:32])
            E_.op("reciprocal", b.t("SMALL")[:, rt:rt + 1], b.t("STAT")[:, 11:12])
            E_.op("tensor_scalar", b.t("SMALL")[:, 64 + rt * 8: 72 + rt * 8],
                  b.t("SMALL")[:, 16:24], b.t("SMALL")[:, rt:rt + 1], None, ALU.mult,
                  inc=("V", 1))
            b.mark(f"{tag}.wteval", "V")

    for e in range(E):
        ffn_expert(b, eng, E_, app, sg, e)

    for tt in range(NTT):
        tag = f"{pre}.fe{tt}"
        par = tt % 2
        xsem = "DX0" if par == 0 else "DX1"
        ssem = "DS0" if par == 0 else "DS1"
        if eng == "sync":
            if tt >= 2:
                E_.wait(f"{pre}.fe{tt-2}.stt")
            elif not first:
                E_.wait(f"{prev}.fe{6 + tt}.stt")
            else:
                E_.wait(pva)
            E_.dma(b.t("AT")[:, par * 1024:(par + 1) * 1024],
                   b.t(src)[t0 + tt * 128: t0 + (tt + 1) * 128, :], inc=(xsem, 16))
            b.mark(f"{tag}.x", xsem)
        if eng == "vector":
            E_.wait(f"{tag}.x")
            E_.op("tensor_add",
                  b.t("AF1")[:, tt * 1024:(tt + 1) * 1024],
                  b.t("AF1")[:, tt * 1024:(tt + 1) * 1024],
                  b.t("AT")[:, par * 1024:(par + 1) * 1024], inc=("V", 1))
            b.mark(f"{tag}.stt", "V")
        if eng == "sync":
            E_.wait(f"{tag}.stt")
            E_.dma(b.t(dst)[t0 + tt * 128: t0 + (tt + 1) * 128, :],
                   b.t("AF1")[:, tt * 1024:(tt + 1) * 1024], inc=(ssem, 16))
            b.mark(f"{tag}.st", ssem)
    if eng == "sync":
        b.mark(f"{pre}.sd0", "DS0")
        b.mark(f"{pre}.sd1", "DS1")


def ffn_expert(b, eng, E_, app, sg, e):
    pre = f"f{app}.{sg}"
    tag = f"{pre}.e{e}"
    first = (sg == 0 and e == 0)
    prev_e = f"{pre}.e{e-1}" if e > 0 else (f"f{app}.{sg-1}.e{E-1}" if sg > 0 else None)

    for ic in range(IC):
        icpar = ic % 2
        P0 = icpar * 4
        wsem = "DW0" if icpar == 0 else "DW1"
        blk = (e * IC + ic) * 128
        if eng == "sync":
            if ic >= 2:
                E_.wait(f"{tag}.mm{ic-2}")
            else:
                if e == 0:
                    E_.wait(f"{pre}.ln{SG // 128 - 1}.rstd")
                if prev_e is not None:
                    E_.wait(f"{prev_e}.mm{IC-2+ic}")
            if app == 1 and sg == 0:
                E_.wait(f"pw.moe.out.{blk // 128 * 3}")
                E_.wait(f"pw.moe.out.{blk // 128 * 3 + 1}")
                if e == 0 and ic < 2 and STAGE_N >= 3:
                    E_.wait("pw.qc.tr")
                    E_.wait(f"pw.ab.out.{30 + ic}")
            E_.dma(b.t("GSTG")[:, icpar * 1024:(icpar + 1) * 1024],
                   b.t("gate_t")[blk: blk + 128, :], inc=(wsem, 16))
            E_.dma(b.t("OSTG")[:, icpar * 1024:(icpar + 1) * 1024],
                   b.t("up_t")[blk: blk + 128, :], inc=(wsem, 16))
            b.mark(f"{tag}.wd{ic}", wsem)
        if eng == "tensor":
            E_.wait(f"{tag}.wd{ic}")
            if ic >= 2:
                E_.wait(f"{tag}.hm{ic-2}")
            elif e == 0:
                E_.wait(f"{pre}.ln{SG // 128 - 1}.lnev")
                if ic == 1:
                    E_.wait(f"{pre}.rt{SG // 128 - 1}.exp")
            else:
                if ic == 0:
                    E_.wait(f"{prev_e}.h1.ev3")
                else:
                    E_.wait(f"{prev_e}.h1.ev7")
            for kc in range(KC):
                st, sp = (kc == 0), (kc == KC - 1)
                Gt = b.t("GSTG")[:, icpar * 1024 + kc * 128: icpar * 1024 + (kc + 1) * 128]
                Ut = b.t("OSTG")[:, icpar * 1024 + kc * 128: icpar * 1024 + (kc + 1) * 128]
                r0 = b.t("AY")[:, kc * 1024: kc * 1024 + 512]
                r1 = b.t("AY")[:, kc * 1024 + 512: kc * 1024 + 1024]
                E_.mm(psf(b, P0 + 0), Gt, r0, start=st, stop=sp)
                E_.mm(psf(b, P0 + 1), Gt, r1, start=st, stop=sp)
                E_.mm(psf(b, P0 + 2), Ut, r0, start=st, stop=sp)
                E_.mm(psf(b, P0 + 3), Ut, r1, start=st, stop=sp,
                      inc=("T", 1) if kc == KC - 1 else None)
            b.mark(f"{tag}.mm{ic}", "T")
        if eng == "scalar":
            for n in range(2):
                E_.wait(f"{tag}.mm{ic}")
                if ic > 0:
                    E_.wait(f"{tag}.hmh{ic-1}.{n}")
                elif prev_e is not None:
                    E_.wait(f"{prev_e}.hmh{IC-1}.{n}")
                elif app == 1:
                    E_.wait("a3.pj.done")
                else:
                    E_.wait("b.done.v")
                E_.op("activation", b.t("WSTG")[:, n * 512:(n + 1) * 512],
                      psf(b, P0 + n), AF.Silu, inc=("S", 1))
                b.mark(f"{tag}.si{ic}.{n}", "S")
        if eng == "vector":
            for n in range(2):
                E_.wait(f"{tag}.si{ic}.{n}")
                if ic == 0 and n == 0:
                    if prev_e is not None:
                        E_.wait(f"{prev_e}.p2end")
                    elif app == 1:
                        E_.wait("a3.q3.n3.mm7")
                    else:
                        E_.wait("b.done.t")
                E_.op("tensor_mul",
                      b.t("AW")[:, ic * 1024 + n * 512: ic * 1024 + (n + 1) * 512],
                      b.t("WSTG")[:, n * 512:(n + 1) * 512],
                      psf(b, P0 + 2 + n),
                      inc=("V", 1))
                b.mark(f"{tag}.hmh{ic}.{n}", "V")
            b.mark(f"{tag}.hm{ic}", "V")

    def p2chain(ns):
        sg_, r = divmod(ns, E * 2 * IC)
        e_, r = divmod(r, 2 * IC)
        H_, ic_ = divmod(r, IC)
        if sg_ < 0:
            return None
        return f"f{app}.{sg_}.e{e_}.h{H_}", ic_

    ns0 = (sg * E + e) * 2 * IC
    for Hh in range(2):
        g = f"{tag}.h{Hh}"
        for ic in range(IC):
            ns = ns0 + Hh * IC + ic
            slot = ns % 4
            dsem = "DG0" if slot % 2 == 0 else "DG1"
            blk = (e * IC + ic) * 128
            if eng == "gpsimd":
                if ns - 4 >= 0:
                    pg, pic = p2chain(ns - 4)
                    E_.wait(f"{pg}.mm{pic}")
                else:
                    E_.wait("a3.pj.vdone" if app == 1 else "b.done.v")
                    E_.wait("a3.pj.done" if app == 1 else "b.done.t")
                if app == 1 and sg == 0:
                    E_.wait(f"pw.moe.out.{blk // 128 * 3 + 2}")
                E_.dma(b.t("AQ")[:, slot * 512:(slot + 1) * 512],
                       b.t("down_t")[blk: blk + 128, Hh * 512:(Hh + 1) * 512],
                       inc=(dsem, 16))
                b.mark(f"{g}.dd{ic}", dsem)
            if eng == "tensor":
                E_.wait(f"{g}.dd{ic}")
                E_.wait(f"{tag}.hm{ic}")
                for tt in range(8):
                    if ic == 0:
                        if Hh == 0:
                            if tt == 0:
                                E_.wait(f"{tag}.hm{IC-2}")
                            elif tt == 4:
                                E_.wait(f"{tag}.hm{IC-1}")
                        else:
                            E_.wait(f"{tag}.h0.ev{tt}")
                    lhs = b.t("AW")[:, ic * 1024 + tt * 128: ic * 1024 + (tt + 1) * 128]
                    E_.mm(psf(b, tt, 0, 512), lhs,
                          b.t("AQ")[:, slot * 512:(slot + 1) * 512],
                          start=(ic == 0), stop=(ic == IC - 1),
                          inc=("T", 1) if tt == 7 else None)
                b.mark(f"{g}.mm{ic}", "T")
                if ic == IC - 1:
                    b.mark(f"{g}.mmend", "T")
        if eng == "vector":
            for tt in range(8):
                E_.wait(f"{g}.mmend")
                if first and Hh == 0 and tt == 0:
                    if app == 1:
                        E_.wait("a3.pj.n3.t2.st")
                        E_.wait("a3.pj.n3.t3.st")
                    else:
                        E_.wait(f"b{NTOK//512-1}.pj.t3.tr")
                if e == 0:
                    E_.wait(f"{pre}.ln{tt}.rstd")
                    if sg > 0 and Hh == 0:
                        E_.wait(f"f{app}.{sg-1}.fe{tt}.st")
                wcol = b.t("SMALL")[:, 64 + tt * 8 + e: 64 + tt * 8 + e + 1]
                ps = psf(b, tt, 0, 512)
                dstv = b.t("AF1")[:, tt * 1024 + Hh * 512: tt * 1024 + (Hh + 1) * 512]
                if e == 0:
                    E_.op("tensor_scalar", dstv, ps, wcol, None, ALU.mult, inc=("V", 1))
                else:
                    E_.op("tensor_scalar", b.t("FT")[:, 0:512], ps, wcol, None, ALU.mult)
                    E_.op("tensor_add", dstv, dstv, b.t("FT")[:, 0:512], inc=("V", 1))
                b.mark(f"{g}.ev{tt}", "V")
    if eng == "tensor":
        b.mark(f"{tag}.p2end", "T")
def phase_reshard(b, eng, E_):
    if eng == "gpsimd":
        gq = 3
        E_.wait(f"f1.{2 * gq + 1}.sd0")
        E_.wait(f"f1.{2 * gq + 1}.sd1")
        if b.real:
            inst = E_.e.collective_compute(
                "AllGather", ALU.bypass,
                replica_groups=[list(range(NCORES))],
                ins=[b.ctx["x2_spill"][gq * 2048:(gq + 1) * 2048, :].opt()],
                outs=[b.ctx[f"x2_ag{gq}"].opt()],
            )
            inst.then_inc(b.sem["C"], 1)
        b.bump("C", 1)
        b.mark(f"rs.g{gq}", "C")
        for gq in range(4):
            E_.wait(f"rs.g{gq}")
            if b.real:
                sl = b.ctx[f"x2_ag{gq}"].rearrange("(r t) d -> r t d", r=8)[
                    :, bass.ds(E_.pid256(), PL), :]
                dst = b.ctx["x2_local"].rearrange("(r al p) d -> r al p d", r=8, al=4)[
                    :, gq, :, :]
            else:
                sl = dst = DUMMY
            E_.dma(dst, sl, inc=("DG", 16))
        b.mark("rs.local", "DG")
        E_.wait("rs.local")
        b.mark("rs.done", "C")


def xall_tile(b, E_, p0):
    if not b.real:
        return DUMMY
    v = b.ctx["x2_local"].rearrange("(a pc) d -> a pc d", a=32)
    return v[:, p0:p0 + 4, :].transpose([1, 0, 2])


def x3_tile(b, p0):
    if not b.real:
        return DUMMY
    v = b.ctx["x3_local"].rearrange("(a pc) d -> a pc d", a=32)
    return v[:, p0:p0 + 4, :].transpose([1, 0, 2])


def phase_attn_b(b, eng, E_):
    NG = NTOK // 512
    if eng == "sync":
        E_.wait("rs.done")
        E_.wait(f"f1.{NSG-1}.e{E-1}.p2end")
        for i, wname in enumerate(["wq_t", "wk_t", "wv_t"]):
            for kc in range(KC):
                E_.dma(b.t("AW")[:, i * 8192 + kc * 1024: i * 8192 + (kc + 1) * 1024],
                       b.t(wname)[kc * 128:(kc + 1) * 128, :], inc=("DW0", 16))
        for kc in range(KC):
            E_.dma(b.t("AQ")[:, kc * 1024:(kc + 1) * 1024],
                   b.t("wo_t")[kc * 128:(kc + 1) * 128, :], inc=("DW0", 16))
        b.mark("b.wload", "DW0")
        E_.wait("b.wload")
    if eng == "gpsimd":
        E_.wait("rs.done")
        E_.wait(f"f1.{NSG-1}.fe7.stt")
        E_.dma(b.t("ROWB")[0:1, 0:D], b.t("bv"), inc=("DG", 16))
        b.mark("b.bvrow", "DG")
        E_.wait("b.bvrow")
    for g in range(NG):
        attn_b_group(b, eng, E_, g)
    if eng == "vector":
        b.mark("b.done.v", "V")
    if eng == "tensor":
        b.mark("b.done.t", "T")
    if eng == "sync":
        b.mark("b.sdone0", "DS0")
        b.mark("b.sdone1", "DS1")


def attn_b_group(b, eng, E_, g):
    pre = f"b{g}"
    prev = f"b{g-1}"
    first = (g == 0)
    NTT = 4

    for tt in range(NTT):
        tag = f"{pre}.ln{tt}"
        par = tt % 2
        xsem = "DX0" if par == 0 else "DX1"
        ATh = b.t("AT")[:, par * 1024:(par + 1) * 1024]
        p0 = g * 16 + tt * 4
        if eng == "sync":
            if tt >= 2:
                E_.wait(f"{pre}.ln{tt-2}.nst")
            elif not first:
                E_.wait(f"{prev}.pj.t{2 + tt}.stt")
            else:
                E_.wait("rs.local")
                E_.wait("f1.7.sd0")
                E_.wait("f1.7.sd1")
            E_.dma(ATh, xall_tile(b, E_, p0), inc=(xsem, 16))
            b.mark(f"{tag}.x", xsem)
        if eng == "scalar":
            E_.wait(f"{tag}.x")
            if tt >= 1:
                E_.wait(f"{pre}.ln{tt-1}.var")
            elif not first:
                E_.wait(f"{prev}.ln{NTT-1}.var")
            gs32 = b.t("GSTG").bitcast(F32)
            if first and tt == 0:
                E_.wait(f"f1.{NSG-1}.e{E-1}.mm{IC-1}")
                E_.wait(f"f1.{NSG-1}.e{E-1}.mm{IC-2}")
            E_.op("activation", gs32[:, 0:512], ATh[:, 0:512],
                  AF.Square, accum_out=b.t("STAT")[:, 0:1])
            E_.op("activation", gs32[:, 512:1024], ATh[:, 512:1024],
                  AF.Square, accum_out=b.t("STAT")[:, 1:2], inc=("S", 1))
            b.mark(f"{tag}.sq", "S")
        if eng == "vector":
            E_.wait(f"{tag}.x")
            if tt >= 1:
                E_.wait(f"{pre}.ln{tt-1}.rstd")
            elif not first:
                E_.wait(f"{prev}.ln{NTT-1}.rstd")
            else:
                E_.wait(f"f1.{NSG-1}.fe7.stt")
            E_.op("reduce_sum", b.t("STAT")[:, 2:3], ATh, axis=AX, inc=("V", 1))
            b.mark(f"{tag}.sum", "V")
            E_.wait(f"{tag}.sq")
            E_.op("tensor_add", b.t("STAT")[:, 3:4], b.t("STAT")[:, 0:1], b.t("STAT")[:, 1:2])
            E_.op("tensor_scalar_mul", b.t("STAT")[:, 4:5], b.t("STAT")[:, 2:3], 1.0 / D)
            E_.op("tensor_scalar_mul", b.t("STAT")[:, 5:6], b.t("STAT")[:, 3:4], 1.0 / D)
            E_.op("tensor_mul", b.t("STAT")[:, 6:7], b.t("STAT")[:, 4:5], b.t("STAT")[:, 4:5])
            E_.op("tensor_sub", b.t("STAT")[:, 7:8], b.t("STAT")[:, 5:6], b.t("STAT")[:, 6:7],
                  inc=("V", 1))
            b.mark(f"{tag}.var", "V")
        if eng == "scalar":
            E_.wait(f"{tag}.var")
            E_.op("activation", b.t("STAT")[:, 8:9], b.t("STAT")[:, 7:8], AF.Ln, bias=1e-5)
            E_.op("activation", b.t("STAT")[:, 9:10], b.t("STAT")[:, 8:9], AF.Exp, scale=-0.5,
                  inc=("S", 1))
            b.mark(f"{tag}.rstd", "S")
        if eng == "vector":
            E_.wait(f"{tag}.rstd")
            if tt >= 1:
                E_.wait(f"{pre}.ln{tt-1}.lntr")
            elif not first:
                E_.wait(f"{prev}.ln{NTT-1}.lntr")
            E_.op("tensor_scalar", b.t("AXS")[:, 0:1024], ATh,
                  b.t("STAT")[:, 4:5], b.t("STAT")[:, 9:10], ALU.subtract, ALU.mult,
                  inc=("V", 1))
            b.mark(f"{tag}.nst", "V")
        if eng == "tensor":
            E_.wait(f"{tag}.nst")
            if first and tt == 0:
                E_.wait(f"f1.{NSG-1}.fe7.stt")
                E_.wait(f"f1.{NSG-1}.fe6.stt")
            if tt >= 1:
                E_.wait(f"{pre}.ln{tt-1}.lnev")
            elif not first:
                E_.wait(f"{prev}.lnevlast")
            for kc2 in range(KC):
                E_.op("transpose",
                      psbf(b, 6, (kc2 % 2) * 128 + (kc2 // 2) * 512, 128),
                      b.t("AXS")[:, kc2 * 128:(kc2 + 1) * 128], b.t("IDB")[:, :],
                      inc=("T", 1) if kc2 == KC - 1 else None)
            b.mark(f"{tag}.lntr", "T")
        if eng == "vector":
            E_.wait(f"{tag}.lntr")
            if first and tt == 0:
                E_.wait("b.wload")
            for kc2 in range(KC):
                E_.op("tensor_copy",
                      b.t("AY")[:, kc2 * 512 + tt * 128: kc2 * 512 + (tt + 1) * 128],
                      psbf(b, 6, (kc2 % 2) * 128 + (kc2 // 2) * 512, 128),
                      inc=("V", 1) if kc2 == KC - 1 else None)
            b.mark(f"{tag}.lnev", "V")
    if eng == "vector":
        b.mark(f"{pre}.lnevlast", "V")

    for jc in range(KC):
        tag = f"{pre}.pq{jc}"
        bk = jc % 2
        if eng == "tensor":
            E_.wait(f"{pre}.ln{NTT-1}.lnev")
            if jc >= 2:
                E_.wait(f"{pre}.pq{jc-2}.ev")
            elif not first:
                E_.wait(f"{prev}.t3.odone")
            for kc in range(KC):
                E_.mm(psf(b, bk, 0, 512),
                      b.t("AW")[:, kc * 1024 + jc * 128: kc * 1024 + jc * 128 + 128],
                      b.t("AY")[:, kc * 512:(kc + 1) * 512],
                      start=(kc == 0), stop=(kc == KC - 1))
            for kc in range(KC):
                E_.mm(psf(b, 2 + bk, 0, 512),
                      b.t("AW")[:, 8192 + kc * 1024 + jc * 128: 8192 + kc * 1024 + jc * 128 + 128],
                      b.t("AY")[:, kc * 512:(kc + 1) * 512],
                      start=(kc == 0), stop=(kc == KC - 1),
                      inc=("T", 1) if kc == KC - 1 else None)
            b.mark(f"{tag}.mm", "T")
        if eng == "vector":
            E_.wait(f"{tag}.mm")
            E_.op("tensor_scalar", b.t("AY")[:, 4096 + jc * 512: 4096 + (jc + 1) * 512],
                  psf(b, bk, 0, 512), b.t("BIASB")[:, jc:jc + 1],
                  b.t("QSCALE")[:, jc:jc + 1], ALU.add, ALU.mult)
            E_.op("tensor_scalar_add", b.t("AY")[:, 8192 + jc * 512: 8192 + (jc + 1) * 512],
                  psf(b, 2 + bk, 0, 512), b.t("BIASB")[:, 8 + jc: 9 + jc],
                  inc=("V", 1))
            b.mark(f"{tag}.ev", "V")
    for tt in range(NTT):
        tag = f"{pre}.pv{tt}"
        if eng == "tensor":
            if tt == 0:
                E_.wait(f"{pre}.pq{KC-1}.ev")
                E_.wait("b.bvrow")
            if tt >= 1:
                E_.wait(f"{pre}.pv{tt-1}.ev")
            for half in range(2):
                for kc in range(KC):
                    E_.mm(psf(b, 4 + 2 * (tt % 2) + half, 0, 512),
                          b.t("AY")[:, kc * 512 + tt * 128: kc * 512 + (tt + 1) * 128],
                          b.t("AW")[:, 16384 + kc * 1024 + half * 512: 16384 + kc * 1024 + (half + 1) * 512],
                          start=(kc == 0), stop=False)
                E_.mm(psf(b, 4 + 2 * (tt % 2) + half, 0, 512),
                      b.t("ONESR")[0:1, 0:128],
                      b.t("ROWB")[0:1, half * 512:(half + 1) * 512],
                      start=False, stop=True,
                      inc=("T", 1))
            b.mark(f"{tag}.mm", "T")
        if eng == "vector":
            E_.wait(f"{tag}.mm")
            if g > 0:
                E_.wait(f"b{g-1}.pj.t{2 + (tt % 2)}.st")
            for half in range(2):
                E_.op("tensor_copy",
                      b.t("APT")[:, tt * 1024 + half * 512: tt * 1024 + (half + 1) * 512],
                      psf(b, 4 + 2 * (tt % 2) + half, 0, 512),
                      inc=("V", 1) if half == 1 else None)
            b.mark(f"{tag}.ev", "V")
    if eng == "vector":
        b.mark(f"{pre}.proj.done", "V")

    for tt in range(NTT):
        attn_b_tile(b, eng, E_, g, tt)

    attn_b_oproj(b, eng, E_, g)


def attn_b_tile(b, eng, E_, g, tt):
    pre = f"b{g}"
    tg = f"{pre}.t{tt}"
    first = (g == 0 and tt == 0)
    col0 = tt * 128

    if eng == "tensor":
        E_.wait(f"{pre}.proj.done")
        if tt >= 1:
            E_.wait(f"{tg[:-1]}{tt-1}.odone")
        elif g > 0:
            E_.wait(f"b{g-1}.pj.pev")
        for hp in range(KC):
            E_.mm(psf(b, hp // 4, (hp % 4) * 128, 128)[0:64, :],
                  b.t("QCB")[:, hp * 64:(hp + 1) * 64],
                  b.t("AY")[:, 8192 + hp * 512 + col0: 8192 + hp * 512 + col0 + 128],
                  inc=("T", 1) if hp == KC - 1 else None)
        b.mark(f"{tg}.lg", "T")
    if eng == "vector":
        E_.wait(f"{tg}.lg")
        if tt >= 1:
            E_.wait(f"{tg[:-1]}{tt-1}.wtr")
        elif g > 0:
            E_.wait(f"b{g-1}.t{3}.wtr")
        for hp in range(KC):
            if hp >= 1:
                E_.wait(f"{tg}.we{hp-1}")
            ps_lg = psf(b, hp // 4, (hp % 4) * 128, 128)
            for hb in range(2):
                r0 = hb * 32
                E_.op("reduce_max",
                      b.t("SMALL")[r0:r0 + 16, 64 + hp * 4: 64 + (hp + 1) * 4]
                      .rearrange("p f -> p f (k)", k=1) if False else
                      b.t("SMALL")[r0:r0 + 16, 64 + hp * 4: 64 + (hp + 1) * 4]
                      .rearrange("p (f k) -> p f k", k=1),
                      ps_lg[r0:r0 + 16, :].rearrange("p (q a) -> p q a", q=4),
                      axis=AX)
            E_.op("tensor_sub",
                  b.t("SMALL")[0:64, 128 + hp * 0:128 + 128][:, 0:128].rearrange(
                      "p (q a) -> p q a", q=4) if False else
                  b.t("SMALL")[0:64, 128:256].rearrange("p (q a) -> p q a", q=4),
                  ps_lg[0:64, :].rearrange("p (q a) -> p q a", q=4),
                  b.t("SMALL")[0:64, 64 + hp * 4: 64 + (hp + 1) * 4]
                  .rearrange("p (f k) -> p f k", k=1).broadcast_to([64, 4, 32]),
                  inc=("V", 1))
            b.mark(f"{tg}.ws{hp}", "V")
        b.mark(f"{tg}.wsub", "V")
    if eng == "scalar":
        for hp in range(KC):
            E_.wait(f"{tg}.ws{hp}")
            if hp == 0:
                if tt >= 1:
                    E_.wait(f"{tg[:-1]}{tt-1}.oall")
                elif g > 0:
                    E_.wait(f"b{g-1}.t3.oall")
            if first and hp == 0:
                E_.wait(f"f1.{NSG-1}.e{E-1}.si{IC-1}.1")
            for hb in range(2):
                r0 = hb * 32
                E_.op("activation",
                      b.t("OSTG")[r0 if hb == 0 else 32: (16 if hb == 0 else 48),
                                  hp * 128:(hp + 1) * 128][0:16, :] if False else
                      b.t("OSTG")[hb * 32: hb * 32 + 16, hp * 128:(hp + 1) * 128],
                      b.t("SMALL")[hb * 32: hb * 32 + 16, 128:256],
                      AF.Exp, inc=("S", 1) if hb == 1 else None)
            b.mark(f"{tg}.we{hp}", "S")
        b.mark(f"{tg}.wexp", "S")
    if eng == "tensor":
        for hp in range(KC):
            E_.wait(f"{tg}.we{hp}")
            if hp >= 1:
                E_.wait(f"{tg}.wte{hp-1}")
            if hp == 0:
                if tt >= 1:
                    E_.wait(f"{tg[:-1]}{tt-1}.dme")
                elif g > 0:
                    E_.wait(f"b{g-1}.t3.dme")
            for hb in range(2):
                h = hp * 2 + hb
                E_.op("transpose", psbf(b, 7, h * 16, 16),
                      b.t("OSTG")[hb * 32: hb * 32 + 16, hp * 128:(hp + 1) * 128],
                      b.t("IDB")[hb * 32: hb * 32 + 16, hb * 32: hb * 32 + 16],
                      inc=("T", 1) if hb == 1 else None)
            b.mark(f"{tg}.wtr{hp}", "T")
        b.mark(f"{tg}.wtr", "T")
    if eng == "vector":
        for hp in range(KC):
            E_.wait(f"{tg}.wtr{hp}")
            if hp == 0:
                if tt >= 1:
                    E_.wait(f"{tg[:-1]}{tt-1}.attall")
                elif g > 0:
                    E_.wait(f"b{g-1}.t3.attall")
            for hb in range(2):
                h = hp * 2 + hb
                E_.op("tensor_copy", b.t("GSTG")[:, h * 16:(h + 1) * 16],
                      psbf(b, 7, h * 16, 16),
                      inc=("V", 1) if hb == 1 else None)
            b.mark(f"{tg}.wte{hp}", "V")
        b.mark(f"{tg}.wtev", "V")
    if eng == "tensor":
        E_.wait(f"{tg}.wtev")
        E_.mm(psf(b, 3, 0, 256), b.t("UM4")[:, :], b.t("GSTG")[:, 0:256], inc=("T", 1))
        b.mark(f"{tg}.dmm", "T")
    if eng == "vector":
        E_.wait(f"{tg}.dmm")
        E_.op("tensor_copy", b.t("FT")[:, 0:256], psf(b, 3, 0, 256))
        E_.op("reciprocal", b.t("FT")[:, 256:512], b.t("FT")[:, 0:256], inc=("V", 1))
        b.mark(f"{tg}.dme", "V")
    for h in range(H):
        th = f"{tg}.h{h}"
        hp, hb = h // 2, h % 2
        if eng == "tensor":
            if h == 0:
                E_.wait(f"{tg}.dme")
            else:
                E_.wait(f"{tg}.h{h-1}.skqe")
            E_.mm(psf(b, 6, 0, 128),
                  b.t("AY")[:, 8192 + hp * 512 + col0: 8192 + hp * 512 + col0 + 128][hb * 64: hb * 64 + 64, :],
                  b.t("AY")[:, 4096 + hp * 512 + col0: 4096 + hp * 512 + col0 + 128][hb * 64: hb * 64 + 64, :],
                  inc=("T", 1))
            b.mark(f"{th}.skq", "T")
        if eng == "vector":
            E_.wait(f"{th}.skq")
            E_.op("tensor_mul", b.t("WSTG")[:, 0:128], psf(b, 6, 0, 128), b.t("UM4")[:, :],
                  inc=("V", 1))
            b.mark(f"{th}.skqe", "V")
        if eng == "tensor":
            E_.wait(f"{th}.skqe")
            E_.mm(psf(b, 2, h * 16, 16), b.t("WSTG")[:, 0:128],
                  b.t("GSTG")[:, h * 16:(h + 1) * 16],
                  inc=("T", 1))
            b.mark(f"{th}.att", "T")
    if eng == "tensor":
        b.mark(f"{tg}.attall", "T")
    if eng == "vector":
        E_.wait(f"{tg}.attall")
        E_.op("tensor_mul", b.t("SMALL")[:, 0:256], psf(b, 2, 0, 256), b.t("FT")[:, 256:512],
              inc=("V", 1))
        b.mark(f"{tg}.attn1", "V")
        E_.wait(f"{tg}.attn1")
        E_.op("reduce_max", b.t("STAT")[:, 0:16].rearrange("p (f k) -> p f k", k=1),
              b.t("SMALL")[:, 0:256].rearrange("p (h c) -> p h c", h=16),
              axis=AX, negate=True, inc=("V", 1))
        b.mark(f"{tg}.attm", "V")
        E_.wait(f"{tg}.attm")
        E_.op("tensor_add", b.t("SMALL")[:, 256:512].rearrange("p (h c) -> p h c", h=16),
              b.t("SMALL")[:, 0:256].rearrange("p (h c) -> p h c", h=16),
              b.t("STAT")[:, 0:16].rearrange("p (f k) -> p f k", k=1)
              .broadcast_to([128, 16, 16]), inc=("V", 1))
        b.mark(f"{tg}.attsub", "V")
    if eng == "scalar":
        E_.wait(f"{tg}.attsub")
        E_.op("activation", b.t("SMALL")[:, 0:256], b.t("SMALL")[:, 256:512], AF.Exp,
              inc=("S", 1))
        b.mark(f"{tg}.pexp", "S")
    if eng == "vector":
        E_.wait(f"{tg}.pexp")
        E_.op("reduce_sum", b.t("STAT")[:, 0:16].rearrange("p (f k) -> p f k", k=1),
              b.t("SMALL")[:, 0:256].rearrange("p (h c) -> p h c", h=16), axis=AX)
        E_.op("reciprocal", b.t("STAT")[:, 0:16], b.t("STAT")[:, 0:16])
        E_.op("tensor_mul", b.t("SMALL")[:, 256:512].rearrange("p (h c) -> p h c", h=16),
              b.t("SMALL")[:, 0:256].rearrange("p (h c) -> p h c", h=16),
              b.t("STAT")[:, 0:16].rearrange("p (f k) -> p f k", k=1)
              .broadcast_to([128, 16, 16]))
        E_.op("tensor_mul", b.t("SMALLB")[:, 0:256], b.t("SMALL")[:, 256:512],
              b.t("FT")[:, 256:512], inc=("V", 1))
        b.mark(f"{tg}.r", "V")
    for h in range(H):
        th = f"{tg}.g{h}"
        hp, hb = h // 2, h % 2
        if eng == "tensor":
            E_.wait(f"{tg}.r")
            if h >= 1:
                E_.wait(f"{tg}.g{h-1}.rte")
            rb = hb * 32
            E_.op("transpose", b.t("PSB")[rb:rb + 16, 7 * 1024: 7 * 1024 + 128],
                  b.t("SMALLB")[:, h * 16:(h + 1) * 16],
                  b.t("IDB")[:, :], tile_position=(0, rb), inc=("T", 1))
            b.mark(f"{th}.rt", "T")
        if eng == "vector":
            E_.wait(f"{th}.rt")
            rb = hb * 32
            E_.op("tensor_copy", b.t("SMALLB")[rb:rb + 16, 256:384],
                  b.t("PSB")[rb:rb + 16, 7 * 1024: 7 * 1024 + 128],
                  inc=("V", 1))
            b.mark(f"{th}.rte", "V")
        if eng == "tensor":
            E_.wait(f"{th}.rte")
            if h == 0:
                E_.wait(f"{tg}.h{H-1}.skqe")
            else:
                E_.wait(f"{tg}.g{h-1}.ge")
            E_.mm(psf(b, 6, 128, 128),
                  b.t("OSTG")[hb * 32: hb * 32 + 16, hp * 128:(hp + 1) * 128],
                  b.t("SMALLB")[hb * 32: hb * 32 + 16, 256:384],
                  inc=("T", 1))
            b.mark(f"{th}.g", "T")
        if eng == "vector":
            E_.wait(f"{th}.g")
            E_.op("tensor_mul", b.t("WSTG")[:, 128:256], psf(b, 6, 128, 128), b.t("UM4")[:, :],
                  inc=("V", 1))
            b.mark(f"{th}.ge", "V")
        if eng == "tensor":
            E_.wait(f"{th}.ge")
            ch = hp
            E_.mm(psf(b, 4 + ch // 4, (ch % 4) * 128, 128)[hb * 64: hb * 64 + 64, :],
                  b.t("APT")[:, tt * 1024 + h * 64: tt * 1024 + (h + 1) * 64],
                  b.t("WSTG")[:, 128:256],
                  inc=("T", 1))
            b.mark(f"{th}.o", "T")
    if eng == "tensor":
        b.mark(f"{tg}.oall", "T")
    if eng == "vector":
        E_.wait(f"{tg}.oall")
        for ch in range(KC):
            E_.op("tensor_copy",
                  b.t("AY")[:, 12288 + ch * 512 + tt * 128: 12288 + ch * 512 + (tt + 1) * 128],
                  psf(b, 4 + ch // 4, (ch % 4) * 128, 128),
                  inc=("V", 1) if ch == KC - 1 else None)
        b.mark(f"{tg}.oev", "V")
    if eng == "vector":
        b.mark(f"{tg}.odone", "V")
    if DBG and STAGE_N == 3 and g == 0 and tt == 0:
        if eng == "gpsimd":
            E_.wait(f"{tg}.odone")
            E_.dma(b.t("dbg_bay"), b.t("AY")[:, :], inc=("DG", 16))
            E_.dma(b.t("dbg_bapt"), b.t("APT")[:, :], inc=("DG", 16))
            E_.dma(b.t("dbg_bostg"), b.t("OSTG")[:, :], inc=("DG", 16))
            E_.dma(b.t("dbg_bgstg"), b.t("GSTG")[:, :], inc=("DG", 16))
            E_.dma(b.t("dbg_bft"), b.t("FT")[:, :], inc=("DG", 16))
            E_.dma(b.t("dbg_bsm"), b.t("SMALL")[:, :], inc=("DG", 16))
            E_.dma(b.t("dbg_bsmb"), b.t("SMALLB")[:, :], inc=("DG", 16))
            E_.dma(b.t("dbg_bx"), b.t("AT")[:, :], inc=("DG", 16))
            E_.dma(b.t("dbg_wkt"), b.t("wk_t"), inc=("DG", 16))
            b.mark("dbgb.done", "DG")
            E_.wait("dbgb.done")
        if eng == "vector":
            E_.wait("dbgb.done")
        if eng == "scalar":
            E_.wait("dbgb.done")


def attn_b_oproj(b, eng, E_, g):
    pre = f"b{g}"
    tg = f"{pre}.pj"
    first = (g == 0)
    if eng == "tensor":
        E_.wait(f"{pre}.t{3}.odone")
        for jc in range(KC):
            for kc in range(KC):
                E_.mm(psf(b, jc),
                      b.t("AQ")[:, kc * 1024 + jc * 128: kc * 1024 + jc * 128 + 128],
                      b.t("AY")[:, 12288 + kc * 512: 12288 + (kc + 1) * 512],
                      start=(kc == 0), stop=(kc == KC - 1),
                      inc=("T", 1) if kc == KC - 1 else None)
            b.mark(f"{tg}.pmm{jc}", "T")
    if eng == "vector":
        E_.wait(f"{tg}.pmm{KC-1}")
        for jc in range(KC):
            E_.op("tensor_scalar_add", b.t("AF1")[:, jc * 512:(jc + 1) * 512],
                  psf(b, jc), b.t("BIASB")[:, 24 + jc: 25 + jc],
                  inc=("V", 1) if jc == KC - 1 else None)
        b.mark(f"{tg}.pev", "V")
    for t4 in range(4):
        tb = f"{tg}.t{t4}"
        tpar = t4 % 2
        xsem = "DX0" if tpar == 0 else "DX1"
        ssem = "DS0" if tpar == 0 else "DS1"
        p0 = g * 16 + t4 * 4
        if eng == "sync":
            if first and t4 < 2:
                E_.wait(f"{pre}.ln{2 + t4}.nst")
                E_.wait(f"{pre}.ln{2 + t4}.rstd")
            elif t4 < 2:
                E_.wait(f"{tg[:-2]}ln{2 + t4}.nst")
                E_.wait(f"{tg[:-2]}ln{2 + t4}.rstd")
            else:
                E_.wait(f"{tg}.t{t4-2}.stt")
            E_.dma(b.t("AT")[:, tpar * 1024:(tpar + 1) * 1024],
                   xall_tile(b, E_, p0), inc=(xsem, 16))
            b.mark(f"{tb}.x", xsem)
        if eng == "tensor":
            E_.wait(f"{tg}.pev") if t4 == 0 else None
            if t4 >= 2:
                E_.wait(f"{tg}.t{t4-2}.stt")
            for jc in range(KC):
                E_.op("transpose",
                      psf(b, tpar * 2 + jc // 4, (jc % 4) * 128, 128),
                      b.t("AF1")[:, jc * 512 + t4 * 128: jc * 512 + t4 * 128 + 128],
                      b.t("IDF")[:, :],
                      inc=("T", 1) if jc == KC - 1 else None)
            b.mark(f"{tb}.tr", "T")
        if eng == "vector":
            E_.wait(f"{tb}.tr")
            E_.wait(f"{tb}.x")
            if g > 0 or t4 >= 2:
                pg, pt = (g, t4 - 2) if t4 >= 2 else (g - 1, t4 + 2)
                E_.wait(f"b{pg}.pj.t{pt}.st")
            elif first and t4 < 2:
                E_.wait(f"f1.{NSG-1}.fe{6 + t4}.st")
            E_.op("scalar_tensor_tensor",
                  aptf(b)[:, tpar * 1024:(tpar + 1) * 1024],
                  b.t("PS")[:, tpar * 1024:(tpar + 1) * 1024],
                  1.0, b.t("AT")[:, tpar * 1024:(tpar + 1) * 1024],
                  ALU.mult, ALU.add, inc=("V", 1))
            b.mark(f"{tb}.stt", "V")
        if eng == "sync":
            E_.wait(f"{tb}.stt")
            E_.dma(x3_tile(b, p0), aptf(b)[:, tpar * 1024:(tpar + 1) * 1024],
                   inc=(ssem, 16))
            b.mark(f"{tb}.st", ssem)


def stage_tap(b, eng, E_):
    if STAGE_N == 1:
        src = "x1_spill"
        need = [f"a{s}.pj.sdone0" for s in range(BL)] + [f"a{s}.pj.sdone1" for s in range(BL)]
    elif STAGE_N == 2:
        src = "x2_spill"
        need = [f"f1.{NSG-1}.sd0", f"f1.{NSG-1}.sd1"]
    elif STAGE_N == 3:
        src = "x3_local"
        need = ["b.sdone0", "b.sdone1"]
    else:
        src = None
        need = [f"f2.{NSG-1}.sd0", f"f2.{NSG-1}.sd1"]
    if eng == "gpsimd":
        for cp in need:
            E_.wait(cp)
        if src is not None:
            for ch in range(8):
                E_.dma(b.t("out")[ch * 1024:(ch + 1) * 1024, :],
                       b.t(src)[ch * 1024:(ch + 1) * 1024, :], inc=("DG", 16))
            b.mark("tap.done", "DG")
            E_.wait("tap.done")


def kernel(**inputs):
    import ml_dtypes
    consts = host_constants()

    b1 = Builder(real=False)
    build_program(b1)
    b2 = Builder(real=True, cps=b1.cp)
    nc = build_program(b2)

    def shaped(a, shape):
        return np.ascontiguousarray(np.asarray(a).reshape(shape))

    base = {
        "W_attn": shaped(inputs["L0_attn_a_W_attn"], (D, 3 * D)),
        "b_attn": shaped(inputs["L0_attn_a_b_attn"], (1, 3 * D)),
        "W_proj": shaped(inputs["L0_attn_a_W_proj"], (D, D)),
        "b_proj": shaped(inputs["L0_attn_a_b_proj"], (1, D)),
        "q_c": shaped(inputs["L0_attn_b_q_c"], (CLEN, D)),
        "beta": shaped(inputs["L0_attn_b_beta"], (1, H)),
        "Wq": shaped(inputs["L0_attn_b_Wq"], (D, D)),
        "bq": shaped(inputs["L0_attn_b_bq"], (1, D)),
        "Wk": shaped(inputs["L0_attn_b_Wk"], (D, D)),
        "bk": shaped(inputs["L0_attn_b_bk"], (1, D)),
        "Wv": shaped(inputs["L0_attn_b_Wv"], (D, D)),
        "bv": shaped(inputs["L0_attn_b_bv"], (1, D)),
        "Wo": shaped(inputs["L0_attn_b_Wo"], (D, D)),
        "bo": shaped(inputs["L0_attn_b_bo"], (1, D)),
        "router_w": shaped(inputs["L0_ffn_router_w"], (E, D)),
        "gate_w": shaped(inputs["L0_ffn_gate_w"], (E * I_FF, D)),
        "up_w": shaped(inputs["L0_ffn_up_w"], (E * I_FF, D)),
        "down_w": shaped(inputs["L0_ffn_down_w"], (E * D, I_FF)),
    }
    base.update(consts)
    xfull = np.asarray(inputs["L0_attn_a_x"])
    in_maps = []
    for c in range(NCORES):
        m = dict(base)
        m["x"] = np.ascontiguousarray(xfull[c * BL:(c + 1) * BL].reshape(NTOK, D))
        in_maps.append(m)

    res = run_bass_kernel_spmd(nc, in_maps, core_ids=list(range(NCORES)))
    outs = [r["out"].reshape(B, PL, D) for r in res.results]
    return np.concatenate(outs, axis=1)
